# version 21
# baseline (speedup 1.0000x reference)
import numpy as np
from contextlib import ExitStack

import concourse.bass as bass
import concourse.mybir as mybir
from concourse.bass_utils import run_bass_kernel_spmd

F32 = mybir.dt.float32
BF16 = mybir.dt.bfloat16
AF = mybir.ActivationFunctionType
OP = mybir.AluOpType

NCORES = 8


class CFG:
    def __init__(self, B=2, L=4096, D=2048, DFF=8192, KLEN=2048, F=4096,
                 TT=256, GSZ=512):
        self.B, self.L, self.D, self.DFF, self.KLEN, self.F = B, L, D, DFF, KLEN, F
        self.TT = TT
        self.GSZ = GSZ
        self.CPB = NCORES // B
        self.NOWN = L // self.CPB
        self.SLAB = self.NOWN + KLEN + 1
        self.SLABP = ((self.SLAB + TT - 1) // TT) * TT
        self.BINS = F // 2 + 1
        self.KC = (self.BINS + 127) // 128
        self.BINSP = 128 * self.KC
        self.PCH = F // 128
        self.p_nz = F - self.NOWN - KLEN + 1
        self.PC0 = self.p_nz // 128
        self.ECH = D // 128
        self.FCH = DFF // 128
        self.TCHO = self.NOWN // 128
        self.GR = D // GSZ
        self.QW = self.NOWN + 2
        self.QCHUNKS = [512] * (self.QW // 512) + (
            [self.QW % 512] if self.QW % 512 else [])
        self.SC = KLEN // 128
        self.R0 = KLEN + 1
        assert D % 128 == 0 and DFF % 128 == 0 and KLEN % 128 == 0
        assert F % 128 == 0 and self.NOWN % 128 == 0 and TT % 128 == 0
        assert F >= self.NOWN + KLEN - 1 and self.p_nz >= 0
        assert self.TCHO <= 8 and GSZ <= 512 and TT <= 512
        assert self.SLABP % 128 == 0 and D % GSZ == 0


def full_cfg():
    return CFG()


def small_cfg():
    return CFG(B=2, L=1024, D=256, DFF=1024, KLEN=256, F=512, TT=256, GSZ=128)



def make_tables(cfg):
    F, BINS, BINSP, NOWN = cfg.F, cfg.BINS, cfg.BINSP, cfg.NOWN
    p = np.arange(F, dtype=np.float64)[:, None]
    k = np.arange(BINSP, dtype=np.float64)[None, :]
    ang = 2.0 * np.pi * p * k / F
    fwd_cos = np.cos(ang)
    fwd_sin = -np.sin(ang)
    fwd_cos[:, BINS:] = 0.0
    fwd_sin[:, BINS:] = 0.0
    kk = np.arange(BINSP, dtype=np.float64)[:, None]
    t = np.arange(NOWN, dtype=np.float64)[None, :]
    ck = np.full((BINSP, 1), 2.0 / F)
    ck[0, 0] = 1.0 / F
    if F // 2 < BINSP:
        ck[F // 2, 0] = 1.0 / F
    ck[BINS:, 0] = 0.0
    ang2 = 2.0 * np.pi * (F - NOWN + t) * kk / F
    inv_cos = ck * np.cos(ang2)
    inv_sin = -ck * np.sin(ang2)
    def lay(t):
        KC, PCH = cfg.KC, cfg.PCH
        return np.ascontiguousarray(
            t.reshape(PCH, 128, KC, 128).transpose(2, 1, 0, 3)
            .reshape(KC, 128, PCH * 128).astype(np.float32))
    return {
        "fwd_cos": lay(fwd_cos),
        "fwd_sin": lay(fwd_sin),
        "inv_cos": inv_cos.astype(np.float32),
        "inv_sin": inv_sin.astype(np.float32),
    }


def prepare_in_maps(cfg, inputs):
    x = np.asarray(inputs["L0_attn_u"], dtype=np.float32)
    tabs = make_tables(cfg)
    com = {
        "wqT": np.ascontiguousarray(np.asarray(inputs["L0_attn_Wq"], np.float32).T),
        "wkT": np.ascontiguousarray(np.asarray(inputs["L0_attn_Wk"], np.float32).T),
        "wvT": np.ascontiguousarray(np.asarray(inputs["L0_attn_Wv"], np.float32).T),
        "woT": np.ascontiguousarray(np.asarray(inputs["L0_attn_Wo"], np.float32).T),
        "w1T": np.ascontiguousarray(np.asarray(inputs["L0_ffn_W1"], np.float32).T),
        "w2T": np.ascontiguousarray(np.asarray(inputs["L0_ffn_W2"], np.float32).T),
        "w1_lay": None,
        "bq": np.asarray(inputs["L0_attn_bq"], np.float32),
        "bk": np.asarray(inputs["L0_attn_bk"], np.float32),
        "bv": np.asarray(inputs["L0_attn_bv"], np.float32),
        "bo_rep": np.ascontiguousarray(
            np.broadcast_to(np.asarray(inputs["L0_attn_bo"], np.float32)[None, :],
                            (128, cfg.D))),
        "cqw": np.ascontiguousarray(
            np.asarray(inputs["L0_attn_convq_w"], np.float32)[:, 0, :].T),
        "ckw": np.ascontiguousarray(
            np.asarray(inputs["L0_attn_convk_w"], np.float32)[:, 0, :].T),
        "cqb": np.asarray(inputs["L0_attn_convq_b"], np.float32),
        "ckb": np.asarray(inputs["L0_attn_convk_b"], np.float32),
        "dskip_rep": np.ascontiguousarray(
            np.broadcast_to(np.asarray(inputs["L0_attn_D_skip"], np.float32)[None, :],
                            (128, cfg.D))),
        "ssmT": np.ascontiguousarray(
            np.asarray(inputs["L0_attn_ssm_kernel"], np.float32).T),
    }
    com["w1_lay"] = np.ascontiguousarray(
        com["w1T"].reshape(cfg.ECH, 128, cfg.FCH, 128)
        .transpose(2, 1, 0, 3).reshape(cfg.DFF, cfg.D))
    del com["w1T"]
    com.update(tabs)
    in_maps = []
    for c in range(NCORES):
        b = c // cfg.CPB
        t0 = cfg.NOWN * (c % cfg.CPB)
        slab = np.zeros((cfg.SLABP, cfg.D), dtype=np.float32)
        lo = t0 - (cfg.KLEN + 1)
        src_lo = max(lo, 0)
        slab[src_lo - lo:(t0 + cfg.NOWN) - lo, :] = x[b, src_lo:t0 + cfg.NOWN, :]
        m = dict(com)
        m["x_slab"] = slab
        in_maps.append(m)
    return in_maps



ENGS = ("sync", "gpsimd", "tensor", "vector", "scalar")


class Rec:
    def __init__(self):
        self.q = {e: [] for e in ENGS}

    def add(self, eng, fn):
        self.q[eng].append(fn)

    def replay(self, nc):
        with nc.Block() as blk:
            for e in ENGS:
                ops = self.q[e]
                if not ops:
                    continue

                def run(proxy, ops=ops):
                    for fn in ops:
                        fn(proxy)
                getattr(blk, e)(run)


class Builder:
    def __init__(self, nc, S):
        self.nc = nc
        self.S = S
        self.c = {}
        self.sems = {}
        self.phase_sems = []
        for name in ("T", "V", "A"):
            self._mk(name)

    def _mk(self, name):
        self.sems[name] = self.S.enter_context(self.nc.semaphore(f"s_{name}"))
        self.c[name] = 0
        return name

    def begin_phase(self):
        self.phase_sems = []

    def dsem(self, name, eng):
        if name not in self.sems:
            self._mk(name)
        self.phase_sems.append((name, eng))
        return name

    def drain(self, rec):
        for name, eng in self.phase_sems:
            if self.c[name]:
                self.wait(rec, eng, name, self.c[name])

    def dma(self, rec, eng, sem, out, in_, transpose=False, slow=False):
        self.c[sem] += 16
        v = self.c[sem]
        h = self.sems[sem]

        def fn(E, out=out, in_=in_, h=h, tr=transpose, slow=slow):
            E.dma_start(out=out, in_=in_, transpose=tr,
                        allow_slow_non_contiguous=slow).then_inc(h, 16)
        rec.add(eng, fn)
        return v

    def wait(self, rec, eng, sem, v):
        h = self.sems[sem]
        rec.add(eng, lambda E, h=h, v=v: E.wait_ge(h, v))

    def tok(self, rec, eng, sem, mk):
        prev = self.c[sem]
        self.c[sem] += 1
        v = self.c[sem]
        h = self.sems[sem]
        if prev:
            rec.add(eng, lambda E, h=h, prev=prev: E.wait_ge(h, prev))
        rec.add(eng, lambda E, mk=mk, h=h: mk(E).then_inc(h, 1))
        return v

    _ENG_SEM = {"vector": "V", "scalar": "A"}

    def run(self, rec, eng, mk):
        sem = self._ENG_SEM.get(eng)
        if sem is not None:
            self.tok(rec, eng, sem, mk)
        else:
            rec.add(eng, mk)



def build_nc(cfg):
    nc = bass.Bass("TRN2", target_bir_lowering=False)
    g = cfg
    D, F = g.D, g.F

    def din(name, shape, dt=F32):
        return nc.dram_tensor(name, list(shape), dt, kind="ExternalInput").ap()

    env = {}
    env["x_slab"] = din("x_slab", (g.SLABP, D))
    for n in ("wqT", "wkT", "wvT", "woT"):
        env[n] = din(n, (D, D))
    env["w1_lay"] = din("w1_lay", (g.DFF, D))
    env["w2T"] = din("w2T", (g.DFF, D))
    for n in ("bq", "bk", "bv", "cqb", "ckb"):
        env[n] = din(n, (D,))
    env["bo_rep"] = din("bo_rep", (128, D))
    env["cqw"] = din("cqw", (3, D))
    env["ckw"] = din("ckw", (3, D))
    env["dskip_rep"] = din("dskip_rep", (128, D))
    env["ssmT"] = din("ssmT", (g.KLEN, D))
    env["fwd_cos"] = din("fwd_cos", (g.KC, 128, g.PCH * 128))
    env["fwd_sin"] = din("fwd_sin", (g.KC, 128, g.PCH * 128))
    env["inv_cos"] = din("inv_cos", (g.BINSP, g.NOWN))
    env["inv_sin"] = din("inv_sin", (g.BINSP, g.NOWN))
    env["out"] = nc.dram_tensor("out", [g.NOWN, D], F32, kind="ExternalOutput").ap()

    def scr(name, shape, dt=BF16):
        return nc.dram_tensor(name, list(shape), dt, kind="Internal").ap()

    env["xn_sp"] = scr("xn_sp", (g.SLABP, D))
    env["kv_sp"] = scr("kv_sp", (D, F))
    env["tblf_sp"] = scr("tblf_sp", (2, g.KC, 128, g.PCH * 128))
    env["tbli_sp"] = scr("tbli_sp", (2, g.BINSP, g.NOWN))
    env["khat_sp"] = scr("khat_sp", (2, g.KC, g.GR, 128, g.GSZ))
    env["attn_sp"] = scr("attn_sp", (g.NOWN, D))
    env["w1_bf"] = scr("w1_bf", (g.DFF, D))
    env["w2_bf"] = scr("w2_bf", (g.DFF, D))

    with ExitStack() as S:
        bld = Builder(nc, S)
        psum = S.enter_context(nc.psum_tensor("ps", [128, 4096], F32))
        env["bank"] = lambda i: psum[:, 512 * i:512 * (i + 1)]
        env["drep"] = S.enter_context(nc.sbuf_tensor("drep", [128, D], BF16))

        _phase1(nc, bld, g, env)
        _phase2a(nc, bld, g, env)
        with ExitStack() as S2:
            env["qtm"] = S2.enter_context(
                nc.sbuf_tensor("qtm", [128, g.TCHO * D], BF16))
            _phase2b(nc, bld, g, env)
            with ExitStack() as S3:
                env["y2tm"] = S3.enter_context(
                    nc.sbuf_tensor("y2tm", [128, g.TCHO * D], BF16))
                _phase3(nc, bld, g, env)
                _phase4(nc, bld, g, env)
        _phase5(nc, bld, g, env)

    return nc


def _phase1(nc, bld, g, env):
    rec = Rec()
    bld.begin_phase()
    D, ECH, SLABP = g.D, g.ECH, g.SLABP
    F, KC, PCH, SC, GSZ, GR = g.F, g.KC, g.PCH, g.SC, g.GSZ, g.GR
    bank = env["bank"]
    NTile = SLABP // 128
    BCH = min(512, D)
    NCH = D // BCH
    gX = [bld.dsem(f"p1gX{i}", "gpsimd") for i in range(2)]
    xS = [bld.dsem(f"p1xS{i}", "sync") for i in range(2)]
    gB = bld.dsem("p0gB", "gpsimd")
    gTI = [bld.dsem(f"p0gTI{i}", "gpsimd") for i in range(2)]
    xTI = [bld.dsem(f"p0xTI{i}", "sync") for i in range(2)]
    gTF = [bld.dsem(f"p0gTF{i}", "gpsimd") for i in range(2)]
    xTF = [bld.dsem(f"p0xTF{i}", "sync") for i in range(2)]
    xKB = [bld.dsem(f"p0xKB{i}", "sync") for i in range(2)]
    xZ = bld.dsem("p0xZ", "sync")

    with ExitStack() as P:
        sb = lambda name, shape, dt, **kw: P.enter_context(
            nc.sbuf_tensor(name, list(shape), dt, **kw))
        xb = [sb(f"p1_x{i}", (128, D), F32) for i in range(2)]
        xn = [sb(f"p1_xn{i}", (128, D), BF16) for i in range(2)]
        st6 = sb("p1_st", (128, 6 * NCH), F32)
        mv = sb("p1_mv", (128, 2), F32)
        ivs = sb("p1_ivs", (128, 2), F32)
        ssm_sb = sb("p0_ssm", (128, SC * D), BF16)
        tf = [sb(f"p0_tf{i}", (128, PCH * 128), BF16) for i in range(2)]
        tib = [sb(f"p0_ti{i}", (128, g.NOWN), BF16) for i in range(2)]
        kb = [sb(f"p0_kb{i}", (128, GSZ), BF16) for i in range(2)]
        zt = sb("p0_zero", (128, D), BF16)

        bld.dma(rec, "gpsimd", gB,
                ssm_sb.rearrange("p (sc d) -> p sc d", sc=SC),
                env["ssmT"].rearrange("(sc sp) d -> sp sc d", sp=128))
        bld.wait(rec, "tensor", gB, bld.c[gB])
        zv = bld.tok(rec, "vector", "V", lambda E: E.memset(zt[:, :], 0.0))
        bld.wait(rec, "sync", "V", zv)
        ZC = zt.shape[1]
        for ec in range(D // 128):
            for c0 in range(0, g.p_nz, ZC):
                c1 = min(c0 + ZC, g.p_nz)
                bld.dma(rec, "sync", xZ,
                        env["kv_sp"][ec * 128:(ec + 1) * 128, c0:c1],
                        zt[:, :c1 - c0], slow=True)

        kjobs = [(kc, ri) for kc in range(KC) for ri in range(2)]
        kstate = {"job": 0, "mm_last": {}, "ev_prev": {}}

        def emit_khat(n):
            for _ in range(n):
                if not kjobs:
                    return
                kc, ri = kjobs.pop(0)
                job = kstate["job"]
                tsrc = env["fwd_cos"] if ri == 0 else env["fwd_sin"]
                par = job % 2
                if job >= 2:
                    bld.wait(rec, "gpsimd", xTF[par], bld.c[xTF[par]])
                    bld.wait(rec, "gpsimd", "T", kstate["mm_last"][job - 2])
                v = bld.dma(rec, "gpsimd", gTF[par], tf[par][:, :], tsrc[kc])
                bld.wait(rec, "sync", gTF[par], v)
                bld.dma(rec, "sync", xTF[par],
                        env["tblf_sp"][ri, kc], tf[par][:, :])
                bld.wait(rec, "tensor", gTF[par], v)
                for gg in range(GR):
                    bnk = 6 + (job * GR + gg) % 2
                    if job * GR + gg >= 2:
                        bld.wait(rec, "tensor", "A",
                                 kstate["ev_prev"][(job * GR + gg) - 2])

                    def p0mm(E, b, par, sc, gg):
                        return E.matmul(
                            bank(b)[:, :GSZ],
                            tf[par][:, sc * 128:(sc + 1) * 128],
                            ssm_sb[:, sc * D + gg * GSZ:sc * D + gg * GSZ + GSZ],
                            start=(sc == 0), stop=(sc == SC - 1))
                    for sc in range(SC - 1):
                        bld.run(rec, "tensor",
                                lambda E, b=bnk, par=par, sc=sc, gg=gg, f=p0mm:
                                f(E, b, par, sc, gg))
                    mt = bld.tok(rec, "tensor", "T",
                                 lambda E, b=bnk, par=par, gg=gg, f=p0mm:
                                 f(E, b, par, SC - 1, gg))
                    bld.wait(rec, "scalar", "T", mt)
                    kpar = (job * GR + gg) % 2
                    bld.wait(rec, "scalar", xKB[kpar], bld.c[xKB[kpar]])
                    ev = bld.tok(rec, "scalar", "A",
                                 lambda E, b=bnk, kpar=kpar:
                                 E.copy(kb[kpar][:, :], bank(b)[:, :GSZ]))
                    kstate["ev_prev"][job * GR + gg] = ev
                    bld.wait(rec, "sync", "A", ev)
                    bld.dma(rec, "sync", xKB[kpar],
                            env["khat_sp"][ri, kc, gg], kb[kpar][:, :])
                kstate["mm_last"][job] = bld.c["T"]
                kstate["job"] = job + 1

        ijobs = list(range(2 * KC))

        def emit_itab(n):
            for _ in range(n):
                if not ijobs:
                    return
                idx = ijobs.pop(0)
                ri, kc = idx // KC, idx % KC
                isrc = env["inv_cos"] if ri == 0 else env["inv_sin"]
                par = idx % 2
                if idx >= 2:
                    bld.wait(rec, "gpsimd", xTI[par], bld.c[xTI[par]])
                v = bld.dma(rec, "gpsimd", gTI[par], tib[par][:, :],
                            isrc[kc * 128:(kc + 1) * 128, :])
                bld.wait(rec, "sync", gTI[par], v)
                bld.dma(rec, "sync", xTI[par],
                        env["tbli_sp"][ri, kc * 128:(kc + 1) * 128, :],
                        tib[par][:, :])

        KPT = (2 * KC + NTile - 1) // NTile

        nrm_tok = {}
        for rt in range(NTile):
            par = rt % 2
            if rt >= 2:
                bld.wait(rec, "gpsimd", "V", nrm_tok[rt - 2])
            vld = bld.dma(rec, "gpsimd", gX[par], xb[par][:, :],
                          env["x_slab"][rt * 128:(rt + 1) * 128, :])
            bld.wait(rec, "vector", gX[par], vld)
            if rt >= 2:
                bld.wait(rec, "vector", xS[par], bld.c[xS[par]])
            for i in range(NCH):
                bld.run(rec, "vector", lambda E, par=par, i=i:
                        E.bn_stats(st6[:, 6 * i:6 * (i + 1)],
                                   xb[par][:, BCH * i:BCH * (i + 1)]))
            bld.run(rec, "vector", lambda E: E.bn_aggr(mv[:, 0:2], st6[:, :]))
            bld.run(rec, "vector", lambda E:
                    E.tensor_scalar_add(ivs[:, 0:1], mv[:, 1:2], 1e-5))
            va = bld.tok(rec, "vector", "V", lambda E:
                         E.reciprocal(ivs[:, 1:2], ivs[:, 0:1]))
            bld.wait(rec, "scalar", "V", va)
            sa = bld.tok(rec, "scalar", "A", lambda E:
                         E.activation(ivs[:, 0:1], ivs[:, 1:2], AF.Sqrt))
            bld.wait(rec, "vector", "A", sa)
            nrm_tok[rt] = bld.tok(rec, "vector", "V", lambda E, par=par:
                                  E.tensor_scalar(xn[par][:, :], xb[par][:, :],
                                                  mv[:, 0:1], ivs[:, 0:1],
                                                  OP.subtract, OP.mult))
            bld.wait(rec, "sync", "V", nrm_tok[rt])
            if rt >= 2:
                bld.wait(rec, "sync", xS[par], bld.c[xS[par]])
            bld.dma(rec, "sync", xS[par],
                    env["xn_sp"][rt * 128:(rt + 1) * 128, :], xn[par][:, :])
            emit_khat(KPT)
            emit_itab(KPT)
        emit_khat(len(kjobs))
        emit_itab(len(ijobs))
        bld.drain(rec)
        rec.replay(nc)


def _phase2a(nc, bld, g, env):
    rec = Rec()
    bld.begin_phase()
    D, ECH, TT, SLABP = g.D, g.ECH, g.TT, g.SLABP
    bank = env["bank"]
    TW = TT + 2
    gB = bld.dsem("p2gB", "gpsimd")
    gXQ = [bld.dsem(f"p2gXQ{i}", "sync") for i in range(2)]
    xKVT = [[bld.dsem(f"p2xKVT{q}{p}", "gpsimd") for p in range(2)]
            for q in range(2)]

    with ExitStack() as P:
        sb = lambda name, shape, dt=BF16, **kw: P.enter_context(
            nc.sbuf_tensor(name, list(shape), dt, **kw))
        wk = sb("p2_wk", (128, ECH * D))
        wv = sb("p2_wv", (128, ECH * D))
        XSP = 2 * TT
        xq = [P.enter_context(nc.sbuf_tensor(
            f"p2_xq{i}", [128, ECH * XSP], BF16, is_transpose_dest=True))
            for i in range(2)]
        kcm = [sb(f"p2_kcm{i}", (128, ECH * TW)) for i in range(2)]
        vcm = [sb(f"p2_vcm{i}", (128, ECH * TT)) for i in range(2)]
        kvo = sb("p2_kvo", (128, ECH * TT))
        cvt = sb("p2_cvt", (128, TT))
        cvt2 = sb("p2_cvt2", (128, TT))
        bks = sb("p2_bk", (128, ECH), F32)
        bvs = sb("p2_bv", (128, ECH), F32)
        cks = sb("p2_ck", (128, 3 * ECH), F32)
        ckbs = sb("p2_ckb", (128, ECH), F32)

        bld.dma(rec, "gpsimd", gB, wk.rearrange("p (hc e) -> p hc e", hc=ECH),
                env["wkT"].rearrange("(hc hp) e -> hp hc e", hp=128))
        bld.dma(rec, "gpsimd", gB, wv.rearrange("p (hc e) -> p hc e", hc=ECH),
                env["wvT"].rearrange("(hc hp) e -> hp hc e", hp=128))
        bld.dma(rec, "gpsimd", gB, bks[:, :],
                env["bk"].rearrange("(ec ep) -> ep ec", ep=128), slow=True)
        bld.dma(rec, "gpsimd", gB, bvs[:, :],
                env["bv"].rearrange("(ec ep) -> ep ec", ep=128), slow=True)
        for j in range(3):
            bld.dma(rec, "gpsimd", gB, cks[:, j * ECH:(j + 1) * ECH],
                    env["ckw"][j].rearrange("(ec ep) -> ep ec", ep=128), slow=True)
        bld.dma(rec, "gpsimd", gB, ckbs[:, :],
                env["ckb"].rearrange("(ec ep) -> ep ec", ep=128), slow=True)
        gB_total = bld.c[gB]

        bld.run(rec, "vector", lambda E:
                E.memset(kcm[0].rearrange("p (ec t) -> p ec t", ec=ECH)[:, :, 0:2],
                         0.0))
        z0 = bld.tok(rec, "vector", "V", lambda E:
                     E.memset(kcm[1].rearrange("p (ec t) -> p ec t",
                                               ec=ECH)[:, :, 0:2], 0.0))

        xq_val = {}
        NTile = SLABP // TT
        ev_tok, kv_tok, mm_tok, xb_val, sp_val = {}, {}, {}, {}, {}
        job = 0
        for ti in range(NTile):
            par = ti % 2
            if ti >= 4 and ti % 2 == 0:
                bld.wait(rec, "sync", "T", mm_tok[ti - 3])
            qpar = (ti // 2) % 2
            if ti % 2 == 0:
                xw = min(XSP, SLABP - ti * TT)
                for hc in range(ECH):
                    vld = bld.dma(rec, "sync", gXQ[qpar],
                                  xq[qpar][:, hc * XSP:hc * XSP + xw],
                                  env["xn_sp"][ti * TT:ti * TT + xw,
                                               hc * 128:(hc + 1) * 128],
                                  transpose=True, slow=True)
                xq_val[ti // 2] = vld
            bld.wait(rec, "tensor", gXQ[qpar], xq_val[ti // 2])

            if ti == 0:
                bld.wait(rec, "tensor", gB, gB_total)
                bld.wait(rec, "scalar", "V", z0)
            if ti >= 2:
                bld.wait(rec, "scalar", "V", kv_tok[ti - 2])
            for wsb, dst, bias, off in ((wk, kcm[par], bks, 2),
                                        (wv, vcm[par], bvs, 0)):
                for ec in range(ECH):
                    bnk = job % 2
                    if job >= 2:
                        bld.wait(rec, "tensor", "A", ev_tok[job - 2])

                    qoff = (ti % 2) * TT

                    def p2mm(E, b, w, hc, qpar, ec, qoff=qoff):
                        return E.matmul(
                            bank(b)[:, :TT],
                            w[:, hc * D + ec * 128:hc * D + ec * 128 + 128],
                            xq[qpar][:, hc * XSP + qoff:hc * XSP + qoff + TT],
                            start=(hc == 0), stop=(hc == ECH - 1))
                    for hc in range(ECH - 1):
                        bld.run(rec, "tensor",
                                lambda E, b=bnk, w=wsb, hc=hc, qpar=qpar, ec=ec,
                                f=p2mm: f(E, b, w, hc, qpar, ec))
                    mt = bld.tok(rec, "tensor", "T",
                                 lambda E, b=bnk, w=wsb, qpar=qpar, ec=ec,
                                 f=p2mm: f(E, b, w, ECH - 1, qpar, ec))
                    bld.wait(rec, "scalar", "T", mt)
                    TWW = TW if dst is kcm[par] else TT
                    ev_tok[job] = bld.tok(
                        rec, "scalar", "A",
                        lambda E, b=bnk, dst=dst, ec=ec, off=off, TWW=TWW, bias=bias:
                        E.activation(dst[:, ec * TWW + off:ec * TWW + off + TT],
                                     bank(b)[:, :TT], AF.Identity,
                                     bias=bias[:, ec:ec + 1], scale=1.0))
                    job += 1
            mm_tok[ti] = bld.c["T"]
            bld.wait(rec, "vector", "A", ev_tok[job - 1])
            if ti >= 1:
                bld.wait(rec, "vector", xKVT[0][(ti - 1) % 2], xb_val[ti - 1][0])
                bld.wait(rec, "vector", xKVT[1][(ti - 1) % 2], xb_val[ti - 1][1])
            for ec in range(ECH):
                base = ec * TW
                bld.run(rec, "vector", lambda E, base=base, ec=ec, par=par:
                        E.tensor_scalar(cvt[:, :TT], kcm[par][:, base:base + TT],
                                        cks[:, ec:ec + 1],
                                        ckbs[:, ec:ec + 1], OP.mult, OP.add))
                bld.run(rec, "vector", lambda E, base=base, ec=ec, par=par:
                        E.scalar_tensor_tensor(cvt2[:, :TT],
                                               kcm[par][:, base + 1:base + 1 + TT],
                                               cks[:, ECH + ec:ECH + ec + 1],
                                               cvt[:, :TT], OP.mult, OP.add))
                bld.run(rec, "vector", lambda E, base=base, ec=ec, par=par:
                        E.scalar_tensor_tensor(cvt[:, :TT],
                                               kcm[par][:, base + 2:base + 2 + TT],
                                               cks[:, 2 * ECH + ec:2 * ECH + ec + 1],
                                               cvt2[:, :TT], OP.mult, OP.add))
                bld.run(rec, "vector", lambda E, ec=ec, par=par:
                        E.tensor_mul(kvo[:, ec * TT:(ec + 1) * TT],
                                     cvt[:, :TT],
                                     vcm[par][:, ec * TT:(ec + 1) * TT]))
            kv_tok[ti] = bld.tok(rec, "vector", "V", lambda E, par=par:
                                 E.tensor_copy(
                                     kcm[1 - par].rearrange(
                                         "p (ec t) -> p ec t", ec=ECH)[:, :, 0:2],
                                     kcm[par].rearrange(
                                         "p (ec t) -> p ec t",
                                         ec=ECH)[:, :, TT:TT + 2]))
            bld.wait(rec, "gpsimd", "V", kv_tok[ti])
            poff = g.F - g.NOWN - g.KLEN - 1
            r0 = ti * TT
            c0, c1 = max(r0, 2), min(r0 + TT, g.SLAB)
            if c1 > c0:
                for ec in range(ECH):
                    qi = ec % 2
                    bld.dma(rec, "gpsimd", xKVT[qi][par],
                            env["kv_sp"][ec * 128:(ec + 1) * 128,
                                         c0 + poff:c1 + poff],
                            kvo[:, ec * TT + (c0 - r0):ec * TT + (c1 - r0)],
                            slow=(c1 - c0 < 16))
            xb_val[ti] = (bld.c[xKVT[0][par]], bld.c[xKVT[1][par]])
        bld.drain(rec)
        rec.replay(nc)


def _phase2b(nc, bld, g, env):
    rec = Rec()
    bld.begin_phase()
    D, ECH, QW, NOWN = g.D, g.ECH, g.QW, g.NOWN
    bank = env["bank"]
    gB = bld.dsem("p2bgB", "gpsimd")
    xQ = bld.dsem("p2bxQ", "sync")
    xQ2 = bld.dsem("p2bxQ2", "sync")
    gWC = [bld.dsem(f"p2bgWC{i}", "gpsimd") for i in range(2)]
    xWC = [bld.dsem(f"p2bxWC{i}", "sync") for i in range(2)]

    with ExitStack() as P:
        sb = lambda name, shape, dt=BF16, **kw: P.enter_context(
            nc.sbuf_tensor(name, list(shape), dt, **kw))
        wq = sb("p2b_wq", (128, ECH * D))
        QWT = ((QW + 15) // 16) * 16
        xq = P.enter_context(nc.sbuf_tensor(
            "p2b_xq", [128, ECH * QWT], BF16, is_transpose_dest=True))
        qpre = sb("p2b_qpre", (128, ECH * QW))
        qcv = sb("p2b_qcv", (128, ECH * NOWN))
        cvt = sb("p2b_cvt", (128, 512), F32)
        cvt2 = sb("p2b_cvt2", (128, 512), F32)
        bqs = sb("p2b_bq", (128, ECH), F32)
        cqs = sb("p2b_cq", (128, 3 * ECH), F32)
        cqbs = sb("p2b_cqb", (128, ECH), F32)
        wcb = [sb(f"p2b_wc{i}", (128, 1024)) for i in range(2)]

        CW2 = min(1024, D)
        w2jobs = [(r, cc) for r in range(g.DFF // 128) for cc in range(D // CW2)]
        w2state = {"n": 0, "val": {0: 0, 1: 0}}

        def emit_w2(n):
            for _ in range(n):
                if not w2jobs:
                    return
                r, cc = w2jobs.pop(0)
                wpar = len(w2jobs) % 2
                if w2state["n"] >= 2:
                    bld.wait(rec, "gpsimd", xWC[wpar], w2state["val"][wpar])
                v = bld.dma(rec, "gpsimd", gWC[wpar], wcb[wpar][:, :CW2],
                            env["w2T"][r * 128:(r + 1) * 128,
                                       cc * CW2:(cc + 1) * CW2])
                bld.wait(rec, "sync", gWC[wpar], v)
                bld.dma(rec, "sync", xWC[wpar],
                        env["w2_bf"][r * 128:(r + 1) * 128,
                                     cc * CW2:(cc + 1) * CW2],
                        wcb[wpar][:, :CW2])
                w2state["val"][wpar] = bld.c[xWC[wpar]]
                w2state["n"] += 1

        bld.dma(rec, "gpsimd", gB, wq.rearrange("p (hc e) -> p hc e", hc=ECH),
                env["wqT"].rearrange("(hc hp) e -> hp hc e", hp=128))
        bld.dma(rec, "gpsimd", gB, bqs[:, :],
                env["bq"].rearrange("(ec ep) -> ep ec", ep=128), slow=True)
        for j in range(3):
            bld.dma(rec, "gpsimd", gB, cqs[:, j * ECH:(j + 1) * ECH],
                    env["cqw"][j].rearrange("(ec ep) -> ep ec", ep=128), slow=True)
        bld.dma(rec, "gpsimd", gB, cqbs[:, :],
                env["cqb"].rearrange("(ec ep) -> ep ec", ep=128), slow=True)
        c0 = g.R0 - 2
        QWT = ((QW + 15) // 16) * 16
        for hc in range(ECH):
            bld.dma(rec, "sync", xQ2, xq[:, hc * QWT:hc * QWT + QWT],
                    env["xn_sp"][c0:c0 + QWT, hc * 128:(hc + 1) * 128],
                    transpose=True, slow=True)
        bld.wait(rec, "tensor", gB, bld.c[gB])
        bld.wait(rec, "tensor", xQ2, bld.c[xQ2])

        ev_tok = {}
        job = 0
        CQ = min(512, NOWN)
        QWT = ((QW + 15) // 16) * 16
        for ec in range(ECH):
            co = 0
            for w in g.QCHUNKS:
                bnk = job % 2
                if job >= 2:
                    bld.wait(rec, "tensor", "A", ev_tok[job - 2])

                def p2bmm(E, b, hc, ec, co, w):
                    return E.matmul(
                        bank(b)[:, :w],
                        wq[:, hc * D + ec * 128:hc * D + ec * 128 + 128],
                        xq[:, hc * QWT + co:hc * QWT + co + w],
                        start=(hc == 0), stop=(hc == ECH - 1))
                for hc in range(ECH - 1):
                    bld.run(rec, "tensor",
                            lambda E, b=bnk, hc=hc, ec=ec, co=co, w=w,
                            f=p2bmm: f(E, b, hc, ec, co, w))
                mt = bld.tok(rec, "tensor", "T",
                             lambda E, b=bnk, ec=ec, co=co, w=w, f=p2bmm:
                             f(E, b, ECH - 1, ec, co, w))
                bld.wait(rec, "scalar", "T", mt)
                ev_tok[job] = bld.tok(
                    rec, "scalar", "A",
                    lambda E, b=bnk, ec=ec, co=co, w=w:
                    E.activation(qpre[:, ec * QW + co:ec * QW + co + w],
                                 bank(b)[:, :w], AF.Identity,
                                 bias=bqs[:, ec:ec + 1], scale=1.0))
                co += w
                job += 1
            bld.wait(rec, "vector", "A", ev_tok[job - 1])
            base = ec * QW
            qv = None
            for jt in range(NOWN // CQ):
                o = jt * CQ
                bld.run(rec, "vector", lambda E, base=base, ec=ec, o=o:
                        E.tensor_scalar(cvt[:, :CQ],
                                        qpre[:, base + o:base + o + CQ],
                                        cqs[:, ec:ec + 1],
                                        cqbs[:, ec:ec + 1], OP.mult, OP.add))
                bld.run(rec, "vector", lambda E, base=base, ec=ec, o=o:
                        E.scalar_tensor_tensor(cvt2[:, :CQ],
                                               qpre[:, base + o + 1:base + o + 1 + CQ],
                                               cqs[:, ECH + ec:ECH + ec + 1],
                                               cvt[:, :CQ], OP.mult, OP.add))
                qv = bld.tok(rec, "vector", "V",
                             lambda E, base=base, ec=ec, o=o:
                             E.scalar_tensor_tensor(
                                 qcv[:, ec * NOWN + o:ec * NOWN + o + CQ],
                                 qpre[:, base + o + 2:base + o + 2 + CQ],
                                 cqs[:, 2 * ECH + ec:2 * ECH + ec + 1],
                                 cvt2[:, :CQ], OP.mult, OP.add))
            bld.wait(rec, "sync", "V", qv)
            for tc in range(g.TCHO):
                bld.dma(rec, "sync", xQ,
                        env["qtm"][:, tc * D + ec * 128:tc * D + ec * 128 + 128],
                        qcv[:, ec * NOWN + tc * 128:ec * NOWN + tc * 128 + 128],
                        transpose=True)
            emit_w2(2)
        emit_w2(len(w2jobs))
        bld.drain(rec)
        rec.replay(nc)


def _phase3(nc, bld, g, env):
    rec = Rec()
    bld.begin_phase()
    D, F, KC, PCH, GSZ, GR, NOWN, TCHO = g.D, g.F, g.KC, g.PCH, g.GSZ, g.GR, \
        g.NOWN, g.TCHO
    bank = env["bank"]
    PCO = PCH - TCHO
    gKV = bld.dsem("p3gKV", "sync")
    TFD = 2
    gTF = [bld.dsem(f"p3gTF{i}", "gpsimd") for i in range(TFD)]
    gKH = [bld.dsem(f"p3gKH{i}", "gpsimd") for i in range(2)]
    TID = 4
    gTI = [bld.dsem(f"p3gTI{i}", "gpsimd") for i in range(TID)]
    gWC = [bld.dsem(f"p3gWC{i}", "gpsimd") for i in range(2)]
    xWC = [bld.dsem(f"p3xWC{i}", "sync") for i in range(2)]
    gDR = bld.dsem("p3gDR", "gpsimd")

    with ExitStack() as P:
        sb = lambda name, shape, dt=BF16: P.enter_context(
            nc.sbuf_tensor(name, list(shape), dt))
        kvg = [P.enter_context(nc.sbuf_tensor(
            f"p3_kvg{i}", [128, PCH * GSZ], BF16, is_transpose_dest=True))
            for i in range(2)]
        tf = [sb(f"p3_tf{i}", (128, PCH * 128)) for i in range(TFD)]
        X = [sb("p3_xr", (128, KC * GSZ)), sb("p3_xi", (128, KC * GSZ))]
        Z = X
        Kh = [[sb(f"p3_kr{i}", (128, GSZ)), sb(f"p3_ki{i}", (128, GSZ))]
              for i in range(2)]
        tmp = sb("p3_tmp", (128, GSZ))
        tmp2 = sb("p3_tmp2", (128, GSZ))
        tmp3 = sb("p3_tmp3", (128, GSZ))
        tmp4 = sb("p3_tmp4", (128, GSZ))
        ti = [sb(f"p3_ti{i}", (128, NOWN)) for i in range(TID)]
        t1 = sb("p3_t1", (128, GSZ))
        t2 = sb("p3_t2", (128, GSZ), F32)
        wcb = [sb(f"p3_wc{i}", (128, 1024)) for i in range(2)]

        bld.dma(rec, "gpsimd", gDR, env["drep"][:, :], env["dskip_rep"][:, :])
        bld.wait(rec, "vector", gDR, 16)
        wjobs = []
        for name, dst, rows, cols in (("w1_lay", "w1_bf", g.DFF, D),):
            CW = min(1024, cols)
            for r in range(rows // 128):
                for cc in range(cols // CW):
                    wjobs.append((name, dst, r, cc, CW))
        wj_seen = [0]
        wc_val = {0: 0, 1: 0}

        def emit_wjobs(n):
            for _ in range(n):
                if not wjobs:
                    return
                name, dst, r, cc, CW = wjobs.pop(0)
                wpar = len(wjobs) % 2
                if wj_seen[0] >= 2:
                    bld.wait(rec, "gpsimd", xWC[wpar], wc_val[wpar])
                v = bld.dma(rec, "gpsimd", gWC[wpar], wcb[wpar][:, :CW],
                            env[name][r * 128:(r + 1) * 128,
                                      cc * CW:(cc + 1) * CW])
                bld.wait(rec, "sync", gWC[wpar], v)
                bld.dma(rec, "sync", xWC[wpar],
                        env[dst][r * 128:(r + 1) * 128, cc * CW:(cc + 1) * CW],
                        wcb[wpar][:, :CW])
                wc_val[wpar] = bld.c[xWC[wpar]]
                wj_seen[0] += 1

        ev_tok, ld_tok, inv_mm = {}, {}, {}
        kv_need = {}
        y2v_hist = {}
        y2v = None
        for gg in range(GR):
            kp3 = gg % 2
            if gg == 0:
                for pc in range(PCH):
                    bld.dma(rec, "sync", gKV,
                            kvg[0][:, pc * GSZ:(pc + 1) * GSZ],
                            env["kv_sp"][0:GSZ, pc * 128:(pc + 1) * 128],
                            transpose=True, slow=True)
            if gg == 0:
                kv_need[0] = bld.c[gKV]
            bld.wait(rec, "tensor", gKV, kv_need[gg])
            job = 0
            for kc in range(KC):
                for ri in range(2):
                    gjob = gg * 2 * KC + job
                    par = gjob % TFD
                    eng = "gpsimd"
                    if gjob - TFD >= 0:
                        bld.wait(rec, eng, "T", ld_tok[gjob - TFD])
                    v = bld.dma(rec, eng, gTF[par],
                                tf[par][:, :], env["tblf_sp"][ri, kc])
                    emit_wjobs(1)
                    bld.wait(rec, "tensor", gTF[par], v)
                    bnk = job % 2
                    if job >= 2:
                        bld.wait(rec, "tensor", "V", ev_tok[(gg, job - 2)])

                    def p3mm(E, b, par, pc, kp3=kp3):
                        return E.matmul(bank(b)[:, :GSZ],
                                        tf[par][:, pc * 128:(pc + 1) * 128],
                                        kvg[kp3][:, pc * GSZ:(pc + 1) * GSZ],
                                        start=(pc == g.PC0), stop=(pc == PCH - 1))
                    for pc in range(g.PC0, PCH - 1):
                        bld.run(rec, "tensor",
                                lambda E, b=bnk, par=par, pc=pc, f=p3mm:
                                f(E, b, par, pc))
                    mt = bld.tok(rec, "tensor", "T",
                                 lambda E, b=bnk, par=par, f=p3mm:
                                 f(E, b, par, PCH - 1))
                    ld_tok[gjob] = mt
                    bld.wait(rec, "vector", "T", mt)
                    ev_tok[(gg, job)] = bld.tok(
                        rec, "vector", "V",
                        lambda E, b=bnk, ri=ri, kc=kc:
                        E.tensor_copy(X[ri][:, kc * GSZ:(kc + 1) * GSZ],
                                      bank(b)[:, :GSZ]))
                    job += 1
            if gg + 1 < GR:
                if gg >= 1:
                    bld.wait(rec, "sync", "V", y2v_hist[gg - 1])
                for pc in range(PCH):
                    bld.dma(rec, "sync", gKV,
                            kvg[(gg + 1) % 2][:, pc * GSZ:(pc + 1) * GSZ],
                            env["kv_sp"][(gg + 1) * GSZ:(gg + 2) * GSZ,
                                         pc * 128:(pc + 1) * 128],
                            transpose=True, slow=True)
                kv_need[gg + 1] = bld.c[gKV]
            pw_tok = None
            pw_done = {}
            for kc in range(KC):
                kp = kc % 2
                if kc >= 2:
                    bld.wait(rec, "gpsimd", "V", pw_done[kc - 2])
                elif gg >= 1:
                    last_same = KC - 1 - ((KC - 1 - kc) % 2)
                    bld.wait(rec, "gpsimd", "V", pw_prev[last_same])
                for ri in range(2):
                    bld.dma(rec, "gpsimd", gKH[kp],
                            Kh[kp][ri][:, :], env["khat_sp"][ri, kc, gg])
                bld.wait(rec, "vector", gKH[kp], bld.c[gKH[kp]])
                s = slice(kc * GSZ, (kc + 1) * GSZ)
                bld.run(rec, "vector", lambda E, s=s, kp=kp:
                        E.tensor_mul(tmp[:, :], X[0][:, s], Kh[kp][0][:, :]))
                bld.run(rec, "vector", lambda E, s=s, kp=kp:
                        E.tensor_mul(tmp2[:, :], X[1][:, s], Kh[kp][1][:, :]))
                bld.run(rec, "vector", lambda E, s=s, kp=kp:
                        E.tensor_mul(tmp3[:, :], X[0][:, s], Kh[kp][1][:, :]))
                bld.run(rec, "vector", lambda E, s=s, kp=kp:
                        E.tensor_mul(tmp4[:, :], X[1][:, s], Kh[kp][0][:, :]))
                bld.run(rec, "vector", lambda E, s=s:
                        E.tensor_sub(Z[0][:, s], tmp[:, :], tmp2[:, :]))
                pw_tok = bld.tok(rec, "vector", "V", lambda E, s=s:
                                 E.tensor_add(Z[1][:, s], tmp3[:, :], tmp4[:, :]))
                pw_done[kc] = pw_tok
            pw_prev = pw_done
            bld.wait(rec, "tensor", "V", pw_tok)
            first = True
            for kc in range(KC):
                for ri in range(2):
                    idx = kc * 2 + ri
                    gidx = gg * 2 * KC + idx
                    par = gidx % TID
                    eng = "gpsimd"
                    if gidx - TID >= 0:
                        bld.wait(rec, eng, "T", inv_mm[gidx - TID])
                    v = bld.dma(rec, eng, gTI[par], ti[par][:, :],
                                env["tbli_sp"][ri, kc * 128:(kc + 1) * 128, :])
                    bld.wait(rec, "tensor", gTI[par], v)
                    last = (kc == KC - 1) and (ri == 1)

                    def pinvmm(E, tc, par=par, ri=ri, kc=kc, first=first,
                               last=last):
                        return E.matmul(bank(tc)[:, :GSZ],
                                        ti[par][:, tc * 128:(tc + 1) * 128],
                                        Z[ri][:, kc * GSZ:(kc + 1) * GSZ],
                                        start=first, stop=last)
                    for tc in range(TCHO - 1):
                        bld.run(rec, "tensor", lambda E, tc=tc, f=pinvmm: f(E, tc))
                    inv_mm[gidx] = bld.tok(
                        rec, "tensor", "T", lambda E, f=pinvmm: f(E, TCHO - 1))
                    first = False
            mtv = bld.c["T"]
            bld.wait(rec, "vector", "T", mtv)
            for tc in range(TCHO):
                kvs = slice((PCO + tc) * GSZ, (PCO + tc + 1) * GSZ)
                bld.run(rec, "vector", lambda E, kvs=kvs, gg=gg, kp3=kp3:
                        E.tensor_mul(t1[:, :], kvg[kp3][:, kvs],
                                     env["drep"][:, gg * GSZ:(gg + 1) * GSZ]))
                bld.run(rec, "vector", lambda E, tc=tc:
                        E.tensor_add(t2[:, :], bank(tc)[:, :GSZ], t1[:, :]))
                qs = slice(tc * D + gg * GSZ, tc * D + (gg + 1) * GSZ)

                def y2mul(E, qs=qs):
                    return E.tensor_mul(env["y2tm"][:, qs], t2[:, :],
                                        env["qtm"][:, qs])
                if tc == TCHO - 1:
                    y2v = bld.tok(rec, "vector", "V", y2mul)
                else:
                    bld.run(rec, "vector", y2mul)
            y2v_hist[gg] = y2v
            y2v_hist[gg] = y2v
            bld.wait(rec, "tensor", "V", y2v)
        emit_wjobs(len(wjobs))
        bld.drain(rec)
        rec.replay(nc)


def _phase4(nc, bld, g, env):
    rec = Rec()
    bld.begin_phase()
    D, ECH, NOWN, TCHO = g.D, g.ECH, g.NOWN, g.TCHO
    bank = env["bank"]
    EOSZ = min(512, D)
    EON = D // EOSZ
    gWO = bld.dsem("p4gWO", "gpsimd")
    xY = bld.dsem("p4xY", "sync")
    xAB = [bld.dsem(f"p4xAB{i}", "sync") for i in range(2)]

    with ExitStack() as P:
        sb = lambda name, shape, dt=BF16, **kw: P.enter_context(
            nc.sbuf_tensor(name, list(shape), dt, **kw))
        y2fm = sb("p4_y2fm", (128, ECH * NOWN), is_transpose_dest=True)
        wo = sb("p4_wo", (128, ECH * D))
        ab = [sb(f"p4_ab{i}", (128, EOSZ)) for i in range(2)]

        bld.dma(rec, "gpsimd", gWO, wo.rearrange("p (hc e) -> p hc e", hc=ECH),
                env["woT"].rearrange("(hc hp) e -> hp hc e", hp=128))
        for tc in range(TCHO):
            for ec in range(ECH):
                bld.dma(rec, "sync", xY,
                        y2fm[:, ec * NOWN + tc * 128:ec * NOWN + tc * 128 + 128],
                        env["y2tm"][:, tc * D + ec * 128:tc * D + ec * 128 + 128],
                        transpose=True)
        bld.wait(rec, "tensor", xY, bld.c[xY])
        bld.wait(rec, "tensor", gWO, bld.c[gWO])
        job = 0
        ev_tok = {}
        for tc in range(TCHO):
            for eo in range(EON):
                bnk = job % 2
                if job >= 2:
                    bld.wait(rec, "tensor", "A", ev_tok[job - 2])

                def p4mm(E, b, hc, tc, eo):
                    return E.matmul(
                        bank(b)[:, :EOSZ],
                        y2fm[:, hc * NOWN + tc * 128:hc * NOWN + tc * 128 + 128],
                        wo[:, hc * D + eo * EOSZ:hc * D + (eo + 1) * EOSZ],
                        start=(hc == 0), stop=(hc == ECH - 1))
                for hc in range(ECH - 1):
                    bld.run(rec, "tensor", lambda E, b=bnk, hc=hc, tc=tc, eo=eo:
                            p4mm(E, b, hc, tc, eo))
                mt = bld.tok(rec, "tensor", "T", lambda E, b=bnk, tc=tc, eo=eo:
                             p4mm(E, b, ECH - 1, tc, eo))
                bld.wait(rec, "scalar", "T", mt)
                kpar = job % 2
                bld.wait(rec, "scalar", xAB[kpar], bld.c[xAB[kpar]])
                ev_tok[job] = bld.tok(rec, "scalar", "A",
                                      lambda E, b=bnk, kpar=kpar:
                                      E.copy(ab[kpar][:, :], bank(b)[:, :EOSZ]))
                bld.wait(rec, "sync", "A", ev_tok[job])
                bld.dma(rec, "sync", xAB[kpar],
                        env["attn_sp"][tc * 128:(tc + 1) * 128,
                                       eo * EOSZ:(eo + 1) * EOSZ],
                        ab[kpar][:, :])
                job += 1
        bld.drain(rec)
        rec.replay(nc)


def _phase5(nc, bld, g, env):
    rec = Rec()
    bld.begin_phase()
    D, DFF, ECH, FCH, NOWN, TCHO = g.D, g.DFF, g.ECH, g.FCH, g.NOWN, g.TCHO
    bank = env["bank"]
    TJ = min(512, NOWN)
    NTJ = NOWN // TJ
    EOSZ = min(512, D)
    EON = D // EOSZ
    gB = bld.dsem("p5gB", "gpsimd")
    xXN = bld.dsem("p5xXN", "sync")
    gW1 = [bld.dsem(f"p5gW1{i}", ("gpsimd", "sync")[i % 2]) for i in range(2)]
    gW2 = [bld.dsem(f"p5gW2{i}", ("gpsimd", "sync")[i % 2]) for i in range(2)]
    gXR = [bld.dsem(f"p5gXR{i}", "gpsimd") for i in range(2)]
    xO = [bld.dsem(f"p5xO{i}", "sync") for i in range(2)]

    with ExitStack() as P:
        sb = lambda name, shape, dt=BF16, **kw: P.enter_context(
            nc.sbuf_tensor(name, list(shape), dt, **kw))
        xn = P.enter_context(nc.sbuf_tensor(
            "p5_xn", [128, ECH * NOWN], BF16, is_transpose_dest=True))
        a = sb("p5_a", (128, FCH * NOWN))
        w1c = [sb(f"p5_w1{i}", (128, ECH * 128)) for i in range(2)]
        w2c = [sb(f"p5_w2{i}", (128, EOSZ)) for i in range(2)]
        ub = sb("p5_u", (128, TJ), F32)
        iib = sb("p5_ii", (128, TJ), mybir.dt.int32)
        mb = sb("p5_m", (128, TJ), F32)
        fb = sb("p5_f", (128, TJ), F32)
        cb = sb("p5_c", (128, TJ), F32)
        zb = sb("p5_z", (128, TJ), F32)
        xr = [sb(f"p5_xr{i}", (128, EOSZ), F32) for i in range(2)]
        ar = [sb(f"p5_ar{i}", (128, EOSZ)) for i in range(2)]
        ob = [sb(f"p5_ob{i}", (128, EOSZ), F32) for i in range(2)]
        brp = sb("p5_bo", (128, D), F32)
        t3 = sb("p5_t3", (128, EOSZ), F32)

        bld.dma(rec, "gpsimd", gB, brp[:, :], env["bo_rep"][:, :])
        for hc in range(ECH):
            bld.dma(rec, "sync", xXN, xn[:, hc * NOWN:(hc + 1) * NOWN],
                    env["xn_sp"][g.R0:g.R0 + NOWN, hc * 128:(hc + 1) * 128],
                    transpose=True, slow=True)
        bld.wait(rec, "tensor", gB, bld.c[gB])
        bld.wait(rec, "tensor", xXN, bld.c[xXN])
        job = 0
        ev_tok, w1_tok = {}, {}
        ga = None
        for fc in range(FCH):
            par = fc % 2
            if fc >= 2:
                bld.wait(rec, ("gpsimd", "sync")[par % 2], "T", w1_tok[fc - 2])
            v = bld.dma(rec, ("gpsimd", "sync")[par % 2], gW1[par],
                        w1c[par][:, :],
                        env["w1_bf"][fc * 128:(fc + 1) * 128, :])
            bld.wait(rec, "tensor", gW1[par], v)
            for tj in range(NTJ):
                bnk = job % 2
                if job >= 2:
                    bld.wait(rec, "tensor", "V", ev_tok[job - 2])

                def p5mm(E, b, par, hc, tj):
                    return E.matmul(
                        bank(b)[:, :TJ],
                        w1c[par][:, hc * 128:(hc + 1) * 128],
                        xn[:, hc * NOWN + tj * TJ:hc * NOWN + (tj + 1) * TJ],
                        start=(hc == 0), stop=(hc == ECH - 1))
                for hc in range(ECH - 1):
                    bld.run(rec, "tensor", lambda E, b=bnk, par=par, hc=hc, tj=tj:
                            p5mm(E, b, par, hc, tj))
                mt = bld.tok(rec, "tensor", "T", lambda E, b=bnk, par=par, tj=tj:
                             p5mm(E, b, par, ECH - 1, tj))
                bld.wait(rec, "vector", "T", mt)
                if ga is not None:
                    bld.wait(rec, "vector", "A", ga)
                ev_tok[job] = bld.tok(rec, "vector", "V", lambda E, b=bnk:
                        E.tensor_scalar(ub[:, :], bank(b)[:, :TJ], 10.0, 4096.5,
                                        OP.mult, OP.add))
                bld.run(rec, "vector", lambda E:
                        E.tensor_copy(iib[:, :], ub[:, :]))
                bld.run(rec, "vector", lambda E:
                        E.tensor_copy(fb[:, :], iib[:, :]))
                bld.run(rec, "vector", lambda E:
                        E.tensor_tensor(cb[:, :], fb[:, :], ub[:, :], OP.is_gt))
                bld.run(rec, "vector", lambda E:
                        E.tensor_sub(mb[:, :], fb[:, :], cb[:, :]))
                bld.run(rec, "vector", lambda E:
                        E.tensor_scalar(cb[:, :], mb[:, :], 4096.0, -32768.0,
                                        OP.subtract, OP.max))
                qv = bld.tok(rec, "vector", "V", lambda E:
                             E.tensor_scalar(zb[:, :], cb[:, :], 32767.0, 0.1,
                                             OP.min, OP.mult))
                bld.wait(rec, "scalar", "V", qv)
                ga = bld.tok(rec, "scalar", "A",
                             lambda E, fc=fc, tj=tj:
                             E.activation(a[:, fc * NOWN + tj * TJ:
                                            fc * NOWN + (tj + 1) * TJ],
                                          zb[:, :], AF.Gelu_apprx_tanh))
                job += 1
            w1_tok[fc] = bld.c["T"]
        bld.wait(rec, "tensor", "A", bld.c["A"])
        ev2, w2_tok, st2 = {}, {}, {}
        for eo in range(EON):
            for fc in range(FCH):
                par = fc % 2
                weng = ("gpsimd", "sync")[par % 2]
                if fc >= 2:
                    bld.wait(rec, weng, "T", w2_tok[(eo, fc - 2)])
                elif eo >= 1:
                    bld.wait(rec, weng, "T", w2_tok[(eo - 1, FCH - 2 + fc)])
                v = bld.dma(rec, weng, gW2[par], w2c[par][:, :],
                            env["w2_bf"][fc * 128:(fc + 1) * 128,
                                         eo * EOSZ:(eo + 1) * EOSZ])
                bld.wait(rec, "tensor", gW2[par], v)
                if fc == 0 and eo >= 1:
                    for tc in range(TCHO):
                        bld.wait(rec, "tensor", "V", ev2[(eo - 1, tc)])

                def w2mm(E, tc, par=par, fc=fc):
                    return E.matmul(bank(tc)[:, :EOSZ],
                                    a[:, fc * NOWN + tc * 128:
                                      fc * NOWN + tc * 128 + 128],
                                    w2c[par][:, :],
                                    start=(fc == 0), stop=(fc == FCH - 1))
                for tc in range(TCHO - 1):
                    bld.run(rec, "tensor", lambda E, tc=tc, f=w2mm: f(E, tc))
                w2_tok[(eo, fc)] = bld.tok(rec, "tensor", "T",
                                           lambda E, f=w2mm: f(E, TCHO - 1))
            mmt = bld.c["T"]
            for tc in range(TCHO):
                par = tc % 2
                r0 = g.R0 + tc * 128
                if tc >= 2:
                    bld.wait(rec, "gpsimd", "V", ev2[(eo, tc - 2)])
                elif eo >= 1:
                    bld.wait(rec, "gpsimd", "V", ev2[(eo - 1, TCHO - 2 + tc)])
                vx2 = bld.dma(rec, "gpsimd", gXR[par], xr[par][:, :],
                              env["x_slab"][r0:r0 + 128,
                                            eo * EOSZ:(eo + 1) * EOSZ])
                va2 = bld.dma(rec, "gpsimd", gXR[par], ar[par][:, :],
                              env["attn_sp"][tc * 128:(tc + 1) * 128,
                                             eo * EOSZ:(eo + 1) * EOSZ])
                bld.wait(rec, "vector", "T", mmt)
                bld.wait(rec, "vector", gXR[par], va2)
                if tc >= 2:
                    bld.wait(rec, "vector", xO[par], bld.c[xO[par]])
                elif eo >= 1:
                    bld.wait(rec, "vector", xO[par], bld.c[xO[par]])
                bld.run(rec, "vector", lambda E, tc=tc, par=par:
                        E.tensor_add(t3[:, :], bank(tc)[:, :EOSZ], xr[par][:, :]))
                bld.run(rec, "vector", lambda E, par=par:
                        E.tensor_add(t3[:, :], t3[:, :], ar[par][:, :]))
                ev2[(eo, tc)] = bld.tok(
                    rec, "vector", "V",
                    lambda E, par=par, eo=eo:
                    E.tensor_add(ob[par][:, :], t3[:, :],
                                 brp[:, eo * EOSZ:(eo + 1) * EOSZ]))
                bld.wait(rec, "sync", "V", ev2[(eo, tc)])
                bld.dma(rec, "sync", xO[par],
                        env["out"][tc * 128:(tc + 1) * 128,
                                   eo * EOSZ:(eo + 1) * EOSZ],
                        ob[par][:, :])
        bld.drain(rec)
        rec.replay(nc)



def kernel(**inputs):
    cfg = full_cfg()
    in_maps = prepare_in_maps(cfg, inputs)
    nc = build_nc(cfg)
    res = run_bass_kernel_spmd(nc, in_maps, core_ids=list(range(NCORES)))
    out = np.empty((cfg.B, cfg.L, cfg.D), dtype=np.float32)
    for c in range(NCORES):
        b = c // cfg.CPB
        t0 = cfg.NOWN * (c % cfg.CPB)
        out[b, t0:t0 + cfg.NOWN, :] = res.results[c]["out"]
    return out


# version 22
# speedup vs baseline: 1.0822x; 1.0822x over previous
import numpy as np
from contextlib import ExitStack

import concourse.bass as bass
import concourse.mybir as mybir
from concourse.bass_utils import run_bass_kernel_spmd

F32 = mybir.dt.float32
BF16 = mybir.dt.bfloat16
AF = mybir.ActivationFunctionType
OP = mybir.AluOpType

NCORES = 8


class CFG:
    def __init__(self, B=2, L=4096, D=2048, DFF=8192, KLEN=2048, F=4096,
                 TT=256, GSZ=512):
        self.B, self.L, self.D, self.DFF, self.KLEN, self.F = B, L, D, DFF, KLEN, F
        self.TT = TT
        self.GSZ = GSZ
        self.CPB = NCORES // B
        self.NOWN = L // self.CPB
        self.SLAB = self.NOWN + KLEN + 1
        self.SLABP = ((self.SLAB + TT - 1) // TT) * TT
        self.BINS = F // 2 + 1
        self.KC = (self.BINS + 127) // 128
        self.BINSP = 128 * self.KC
        self.PCH = F // 128
        self.p_nz = F - self.NOWN - KLEN + 1
        self.PC0 = self.p_nz // 128
        self.ECH = D // 128
        self.FCH = DFF // 128
        self.TCHO = self.NOWN // 128
        self.GR = D // GSZ
        self.QW = self.NOWN + 2
        self.QCHUNKS = [512] * (self.QW // 512) + (
            [self.QW % 512] if self.QW % 512 else [])
        self.SC = KLEN // 128
        self.R0 = KLEN + 1
        assert D % 128 == 0 and DFF % 128 == 0 and KLEN % 128 == 0
        assert F % 128 == 0 and self.NOWN % 128 == 0 and TT % 128 == 0
        assert F >= self.NOWN + KLEN - 1 and self.p_nz >= 0
        assert self.TCHO <= 8 and GSZ <= 512 and TT <= 512
        assert self.SLABP % 128 == 0 and D % GSZ == 0


def full_cfg():
    return CFG()


def small_cfg():
    return CFG(B=2, L=1024, D=256, DFF=1024, KLEN=256, F=512, TT=256, GSZ=128)



def make_tables(cfg):
    F, BINS, BINSP, NOWN = cfg.F, cfg.BINS, cfg.BINSP, cfg.NOWN
    p = np.arange(F, dtype=np.float64)[:, None]
    k = np.arange(BINSP, dtype=np.float64)[None, :]
    ang = 2.0 * np.pi * p * k / F
    fwd_cos = np.cos(ang)
    fwd_sin = -np.sin(ang)
    fwd_cos[:, BINS:] = 0.0
    fwd_sin[:, BINS:] = 0.0
    kk = np.arange(BINSP, dtype=np.float64)[:, None]
    t = np.arange(NOWN, dtype=np.float64)[None, :]
    ck = np.full((BINSP, 1), 2.0 / F)
    ck[0, 0] = 1.0 / F
    if F // 2 < BINSP:
        ck[F // 2, 0] = 1.0 / F
    ck[BINS:, 0] = 0.0
    ang2 = 2.0 * np.pi * (F - NOWN + t) * kk / F
    inv_cos = ck * np.cos(ang2)
    inv_sin = -ck * np.sin(ang2)
    def lay(t):
        KC, PCH = cfg.KC, cfg.PCH
        return np.ascontiguousarray(
            t.reshape(PCH, 128, KC, 128).transpose(2, 1, 0, 3)
            .reshape(KC, 128, PCH * 128).astype(np.float32))
    return {
        "fwd_cos": lay(fwd_cos),
        "fwd_sin": lay(fwd_sin),
        "inv_cos": inv_cos.astype(np.float32),
        "inv_sin": inv_sin.astype(np.float32),
    }


def prepare_in_maps(cfg, inputs):
    x = np.asarray(inputs["L0_attn_u"], dtype=np.float32)
    tabs = make_tables(cfg)
    com = {
        "wqT": np.ascontiguousarray(np.asarray(inputs["L0_attn_Wq"], np.float32).T),
        "wkT": np.ascontiguousarray(np.asarray(inputs["L0_attn_Wk"], np.float32).T),
        "wvT": np.ascontiguousarray(np.asarray(inputs["L0_attn_Wv"], np.float32).T),
        "woT": np.ascontiguousarray(np.asarray(inputs["L0_attn_Wo"], np.float32).T),
        "w1T": np.ascontiguousarray(np.asarray(inputs["L0_ffn_W1"], np.float32).T),
        "w2T": np.ascontiguousarray(np.asarray(inputs["L0_ffn_W2"], np.float32).T),
        "w1_lay": None,
        "bq": np.asarray(inputs["L0_attn_bq"], np.float32),
        "bk": np.asarray(inputs["L0_attn_bk"], np.float32),
        "bv": np.asarray(inputs["L0_attn_bv"], np.float32),
        "bo_rep": np.ascontiguousarray(
            np.broadcast_to(np.asarray(inputs["L0_attn_bo"], np.float32)[None, :],
                            (128, cfg.D))),
        "cqw": np.ascontiguousarray(
            np.asarray(inputs["L0_attn_convq_w"], np.float32)[:, 0, :].T),
        "ckw": np.ascontiguousarray(
            np.asarray(inputs["L0_attn_convk_w"], np.float32)[:, 0, :].T),
        "cqb": np.asarray(inputs["L0_attn_convq_b"], np.float32),
        "ckb": np.asarray(inputs["L0_attn_convk_b"], np.float32),
        "dskip_rep": np.ascontiguousarray(
            np.broadcast_to(np.asarray(inputs["L0_attn_D_skip"], np.float32)[None, :],
                            (128, cfg.D))),
        "ssmT": np.ascontiguousarray(
            np.asarray(inputs["L0_attn_ssm_kernel"], np.float32).T),
    }
    com["w1_lay"] = np.ascontiguousarray(
        com["w1T"].reshape(cfg.ECH, 128, cfg.FCH, 128)
        .transpose(2, 1, 0, 3).reshape(cfg.DFF, cfg.D))
    del com["w1T"]
    com.update(tabs)
    in_maps = []
    for c in range(NCORES):
        b = c // cfg.CPB
        t0 = cfg.NOWN * (c % cfg.CPB)
        slab = np.zeros((cfg.SLABP, cfg.D), dtype=np.float32)
        lo = t0 - (cfg.KLEN + 1)
        src_lo = max(lo, 0)
        slab[src_lo - lo:(t0 + cfg.NOWN) - lo, :] = x[b, src_lo:t0 + cfg.NOWN, :]
        m = dict(com)
        m["x_slab"] = slab
        in_maps.append(m)
    return in_maps



ENGS = ("sync", "gpsimd", "tensor", "vector", "scalar")


class Rec:
    def __init__(self):
        self.q = {e: [] for e in ENGS}

    def add(self, eng, fn):
        self.q[eng].append(fn)

    def replay(self, nc):
        with nc.Block() as blk:
            for e in ENGS:
                ops = self.q[e]
                if not ops:
                    continue

                def run(proxy, ops=ops):
                    for fn in ops:
                        fn(proxy)
                getattr(blk, e)(run)


class Builder:
    def __init__(self, nc, S):
        self.nc = nc
        self.S = S
        self.c = {}
        self.sems = {}
        self.phase_sems = []
        for name in ("T", "V", "A"):
            self._mk(name)

    def _mk(self, name):
        self.sems[name] = self.S.enter_context(self.nc.semaphore(f"s_{name}"))
        self.c[name] = 0
        return name

    def begin_phase(self):
        self.phase_sems = []

    def dsem(self, name, eng):
        if name not in self.sems:
            self._mk(name)
        self.phase_sems.append((name, eng))
        return name

    def drain(self, rec):
        for name, eng in self.phase_sems:
            if self.c[name]:
                self.wait(rec, eng, name, self.c[name])

    def dma(self, rec, eng, sem, out, in_, transpose=False, slow=False):
        self.c[sem] += 16
        v = self.c[sem]
        h = self.sems[sem]

        def fn(E, out=out, in_=in_, h=h, tr=transpose, slow=slow):
            E.dma_start(out=out, in_=in_, transpose=tr,
                        allow_slow_non_contiguous=slow).then_inc(h, 16)
        rec.add(eng, fn)
        return v

    def wait(self, rec, eng, sem, v):
        h = self.sems[sem]
        rec.add(eng, lambda E, h=h, v=v: E.wait_ge(h, v))

    def tok(self, rec, eng, sem, mk):
        prev = self.c[sem]
        self.c[sem] += 1
        v = self.c[sem]
        h = self.sems[sem]
        if prev:
            rec.add(eng, lambda E, h=h, prev=prev: E.wait_ge(h, prev))
        rec.add(eng, lambda E, mk=mk, h=h: mk(E).then_inc(h, 1))
        return v

    _ENG_SEM = {"vector": "V", "scalar": "A"}

    def run(self, rec, eng, mk):
        sem = self._ENG_SEM.get(eng)
        if sem is not None:
            self.tok(rec, eng, sem, mk)
        else:
            rec.add(eng, mk)



def build_nc(cfg):
    nc = bass.Bass("TRN2", target_bir_lowering=False)
    g = cfg
    D, F = g.D, g.F

    def din(name, shape, dt=F32):
        return nc.dram_tensor(name, list(shape), dt, kind="ExternalInput").ap()

    env = {}
    env["x_slab"] = din("x_slab", (g.SLABP, D))
    for n in ("wqT", "wkT", "wvT", "woT"):
        env[n] = din(n, (D, D))
    env["w1_lay"] = din("w1_lay", (g.DFF, D))
    env["w2T"] = din("w2T", (g.DFF, D))
    for n in ("bq", "bk", "bv", "cqb", "ckb"):
        env[n] = din(n, (D,))
    env["bo_rep"] = din("bo_rep", (128, D))
    env["cqw"] = din("cqw", (3, D))
    env["ckw"] = din("ckw", (3, D))
    env["dskip_rep"] = din("dskip_rep", (128, D))
    env["ssmT"] = din("ssmT", (g.KLEN, D))
    env["fwd_cos"] = din("fwd_cos", (g.KC, 128, g.PCH * 128))
    env["fwd_sin"] = din("fwd_sin", (g.KC, 128, g.PCH * 128))
    env["inv_cos"] = din("inv_cos", (g.BINSP, g.NOWN))
    env["inv_sin"] = din("inv_sin", (g.BINSP, g.NOWN))
    env["out"] = nc.dram_tensor("out", [g.NOWN, D], F32, kind="ExternalOutput").ap()

    def scr(name, shape, dt=BF16):
        return nc.dram_tensor(name, list(shape), dt, kind="Internal").ap()

    env["xn_sp"] = scr("xn_sp", (g.SLABP, D))
    env["kv_sp"] = scr("kv_sp", (D, F))
    env["tblf_sp"] = scr("tblf_sp", (2, g.KC, 128, g.PCH * 128))
    env["tbli_sp"] = scr("tbli_sp", (2, g.BINSP, g.NOWN))
    env["khat_sp"] = scr("khat_sp", (2, g.KC, g.GR, 128, g.GSZ))
    env["attn_sp"] = scr("attn_sp", (g.NOWN, D))
    env["w1_bf"] = scr("w1_bf", (g.DFF, D))
    env["w2_bf"] = scr("w2_bf", (g.DFF, D))

    with ExitStack() as S:
        bld = Builder(nc, S)
        psum = S.enter_context(nc.psum_tensor("ps", [128, 4096], F32))
        env["bank"] = lambda i: psum[:, 512 * i:512 * (i + 1)]
        env["drep"] = S.enter_context(nc.sbuf_tensor("drep", [128, D], BF16))

        _phase1(nc, bld, g, env)
        _phase2a(nc, bld, g, env)
        with ExitStack() as S2:
            env["qtm"] = S2.enter_context(
                nc.sbuf_tensor("qtm", [128, g.TCHO * D], BF16))
            _phase2b(nc, bld, g, env)
            with ExitStack() as S3:
                env["y2tm"] = S3.enter_context(
                    nc.sbuf_tensor("y2tm", [128, g.TCHO * D], BF16))
                _phase3(nc, bld, g, env)
                _phase4(nc, bld, g, env)
        _phase5(nc, bld, g, env)

    return nc


def _phase1(nc, bld, g, env):
    rec = Rec()
    bld.begin_phase()
    D, ECH, SLABP = g.D, g.ECH, g.SLABP
    F, KC, PCH, SC, GSZ, GR = g.F, g.KC, g.PCH, g.SC, g.GSZ, g.GR
    bank = env["bank"]
    NTile = SLABP // 128
    BCH = min(512, D)
    NCH = D // BCH
    gX = [bld.dsem(f"p1gX{i}", "gpsimd") for i in range(2)]
    xS = [bld.dsem(f"p1xS{i}", "sync") for i in range(2)]
    gB = bld.dsem("p0gB", "gpsimd")
    gTI = [bld.dsem(f"p0gTI{i}", "gpsimd") for i in range(2)]
    xTI = [bld.dsem(f"p0xTI{i}", "sync") for i in range(2)]
    gTF = [bld.dsem(f"p0gTF{i}", "gpsimd") for i in range(2)]
    xTF = [bld.dsem(f"p0xTF{i}", "sync") for i in range(2)]
    xKB = [bld.dsem(f"p0xKB{i}", "sync") for i in range(2)]
    xZ = bld.dsem("p0xZ", "sync")

    with ExitStack() as P:
        sb = lambda name, shape, dt, **kw: P.enter_context(
            nc.sbuf_tensor(name, list(shape), dt, **kw))
        xb = [sb(f"p1_x{i}", (128, D), F32) for i in range(2)]
        xn = [sb(f"p1_xn{i}", (128, D), BF16) for i in range(2)]
        st6 = sb("p1_st", (128, 6 * NCH), F32)
        mv = sb("p1_mv", (128, 2), F32)
        ivs = sb("p1_ivs", (128, 2), F32)
        ssm_sb = sb("p0_ssm", (128, SC * D), BF16)
        tf = [sb(f"p0_tf{i}", (128, PCH * 128), BF16) for i in range(2)]
        tib = [sb(f"p0_ti{i}", (128, g.NOWN), BF16) for i in range(2)]
        kb = [sb(f"p0_kb{i}", (128, GSZ), BF16) for i in range(2)]
        zt = sb("p0_zero", (128, D), BF16)

        bld.dma(rec, "gpsimd", gB,
                ssm_sb.rearrange("p (sc d) -> p sc d", sc=SC),
                env["ssmT"].rearrange("(sc sp) d -> sp sc d", sp=128))
        bld.wait(rec, "tensor", gB, bld.c[gB])
        zv = bld.tok(rec, "vector", "V", lambda E: E.memset(zt[:, :], 0.0))
        bld.wait(rec, "sync", "V", zv)
        ZC = zt.shape[1]
        for ec in range(D // 128):
            for c0 in range(0, g.p_nz, ZC):
                c1 = min(c0 + ZC, g.p_nz)
                bld.dma(rec, "sync", xZ,
                        env["kv_sp"][ec * 128:(ec + 1) * 128, c0:c1],
                        zt[:, :c1 - c0], slow=True)

        kjobs = [(kc, ri) for kc in range(KC) for ri in range(2)]
        kstate = {"job": 0, "mm_last": {}, "ev_prev": {}}

        def emit_khat(n):
            for _ in range(n):
                if not kjobs:
                    return
                kc, ri = kjobs.pop(0)
                job = kstate["job"]
                tsrc = env["fwd_cos"] if ri == 0 else env["fwd_sin"]
                par = job % 2
                if job >= 2:
                    bld.wait(rec, "gpsimd", xTF[par], bld.c[xTF[par]])
                    bld.wait(rec, "gpsimd", "T", kstate["mm_last"][job - 2])
                v = bld.dma(rec, "gpsimd", gTF[par], tf[par][:, :], tsrc[kc])
                bld.wait(rec, "sync", gTF[par], v)
                bld.dma(rec, "sync", xTF[par],
                        env["tblf_sp"][ri, kc], tf[par][:, :])
                bld.wait(rec, "tensor", gTF[par], v)
                for gg in range(GR):
                    bnk = 6 + (job * GR + gg) % 2
                    if job * GR + gg >= 2:
                        bld.wait(rec, "tensor", "A",
                                 kstate["ev_prev"][(job * GR + gg) - 2])

                    def p0mm(E, b, par, sc, gg):
                        return E.matmul(
                            bank(b)[:, :GSZ],
                            tf[par][:, sc * 128:(sc + 1) * 128],
                            ssm_sb[:, sc * D + gg * GSZ:sc * D + gg * GSZ + GSZ],
                            start=(sc == 0), stop=(sc == SC - 1))
                    for sc in range(SC - 1):
                        bld.run(rec, "tensor",
                                lambda E, b=bnk, par=par, sc=sc, gg=gg, f=p0mm:
                                f(E, b, par, sc, gg))
                    mt = bld.tok(rec, "tensor", "T",
                                 lambda E, b=bnk, par=par, gg=gg, f=p0mm:
                                 f(E, b, par, SC - 1, gg))
                    bld.wait(rec, "scalar", "T", mt)
                    kpar = (job * GR + gg) % 2
                    bld.wait(rec, "scalar", xKB[kpar], bld.c[xKB[kpar]])
                    ev = bld.tok(rec, "scalar", "A",
                                 lambda E, b=bnk, kpar=kpar:
                                 E.copy(kb[kpar][:, :], bank(b)[:, :GSZ]))
                    kstate["ev_prev"][job * GR + gg] = ev
                    bld.wait(rec, "sync", "A", ev)
                    bld.dma(rec, "sync", xKB[kpar],
                            env["khat_sp"][ri, kc, gg], kb[kpar][:, :])
                kstate["mm_last"][job] = bld.c["T"]
                kstate["job"] = job + 1

        ijobs = list(range(2 * KC))

        def emit_itab(n):
            for _ in range(n):
                if not ijobs:
                    return
                idx = ijobs.pop(0)
                ri, kc = idx // KC, idx % KC
                isrc = env["inv_cos"] if ri == 0 else env["inv_sin"]
                par = idx % 2
                if idx >= 2:
                    bld.wait(rec, "gpsimd", xTI[par], bld.c[xTI[par]])
                v = bld.dma(rec, "gpsimd", gTI[par], tib[par][:, :],
                            isrc[kc * 128:(kc + 1) * 128, :])
                bld.wait(rec, "sync", gTI[par], v)
                bld.dma(rec, "sync", xTI[par],
                        env["tbli_sp"][ri, kc * 128:(kc + 1) * 128, :],
                        tib[par][:, :])

        KPT = (2 * KC + NTile - 1) // NTile

        nrm_tok = {}
        for rt in range(NTile):
            par = rt % 2
            if rt >= 2:
                bld.wait(rec, "gpsimd", "V", nrm_tok[rt - 2])
            vld = bld.dma(rec, "gpsimd", gX[par], xb[par][:, :],
                          env["x_slab"][rt * 128:(rt + 1) * 128, :])
            bld.wait(rec, "vector", gX[par], vld)
            if rt >= 2:
                bld.wait(rec, "vector", xS[par], bld.c[xS[par]])
            for i in range(NCH):
                bld.run(rec, "vector", lambda E, par=par, i=i:
                        E.bn_stats(st6[:, 6 * i:6 * (i + 1)],
                                   xb[par][:, BCH * i:BCH * (i + 1)]))
            bld.run(rec, "vector", lambda E: E.bn_aggr(mv[:, 0:2], st6[:, :]))
            bld.run(rec, "vector", lambda E:
                    E.tensor_scalar_add(ivs[:, 0:1], mv[:, 1:2], 1e-5))
            va = bld.tok(rec, "vector", "V", lambda E:
                         E.reciprocal(ivs[:, 1:2], ivs[:, 0:1]))
            bld.wait(rec, "scalar", "V", va)
            sa = bld.tok(rec, "scalar", "A", lambda E:
                         E.activation(ivs[:, 0:1], ivs[:, 1:2], AF.Sqrt))
            bld.wait(rec, "vector", "A", sa)
            nrm_tok[rt] = bld.tok(rec, "vector", "V", lambda E, par=par:
                                  E.tensor_scalar(xn[par][:, :], xb[par][:, :],
                                                  mv[:, 0:1], ivs[:, 0:1],
                                                  OP.subtract, OP.mult))
            bld.wait(rec, "sync", "V", nrm_tok[rt])
            if rt >= 2:
                bld.wait(rec, "sync", xS[par], bld.c[xS[par]])
            bld.dma(rec, "sync", xS[par],
                    env["xn_sp"][rt * 128:(rt + 1) * 128, :], xn[par][:, :])
            emit_khat(KPT)
            emit_itab(KPT)
        emit_khat(len(kjobs))
        emit_itab(len(ijobs))
        bld.drain(rec)
        rec.replay(nc)


def _phase2a(nc, bld, g, env):
    rec = Rec()
    bld.begin_phase()
    D, ECH, TT, SLABP = g.D, g.ECH, g.TT, g.SLABP
    bank = env["bank"]
    TW = TT + 2
    gB = bld.dsem("p2gB", "gpsimd")
    gXQ = [bld.dsem(f"p2gXQ{i}", "sync") for i in range(2)]
    xKVT = [[bld.dsem(f"p2xKVT{q}{p}", "gpsimd") for p in range(2)]
            for q in range(2)]

    with ExitStack() as P:
        sb = lambda name, shape, dt=BF16, **kw: P.enter_context(
            nc.sbuf_tensor(name, list(shape), dt, **kw))
        wk = sb("p2_wk", (128, ECH * D))
        wv = sb("p2_wv", (128, ECH * D))
        XSP = 2 * TT
        xq = [P.enter_context(nc.sbuf_tensor(
            f"p2_xq{i}", [128, ECH * XSP], BF16, is_transpose_dest=True))
            for i in range(2)]
        kcm = [sb(f"p2_kcm{i}", (128, ECH * TW)) for i in range(2)]
        vcm = [sb(f"p2_vcm{i}", (128, ECH * TT)) for i in range(2)]
        kvo = sb("p2_kvo", (128, ECH * TT))
        cvt = sb("p2_cvt", (128, TT))
        cvt2 = sb("p2_cvt2", (128, TT))
        bks = sb("p2_bk", (128, ECH), F32)
        bvs = sb("p2_bv", (128, ECH), F32)
        cks = sb("p2_ck", (128, 3 * ECH), F32)
        ckbs = sb("p2_ckb", (128, ECH), F32)

        bld.dma(rec, "gpsimd", gB, wk.rearrange("p (hc e) -> p hc e", hc=ECH),
                env["wkT"].rearrange("(hc hp) e -> hp hc e", hp=128))
        bld.dma(rec, "gpsimd", gB, wv.rearrange("p (hc e) -> p hc e", hc=ECH),
                env["wvT"].rearrange("(hc hp) e -> hp hc e", hp=128))
        bld.dma(rec, "gpsimd", gB, bks[:, :],
                env["bk"].rearrange("(ec ep) -> ep ec", ep=128), slow=True)
        bld.dma(rec, "gpsimd", gB, bvs[:, :],
                env["bv"].rearrange("(ec ep) -> ep ec", ep=128), slow=True)
        for j in range(3):
            bld.dma(rec, "gpsimd", gB, cks[:, j * ECH:(j + 1) * ECH],
                    env["ckw"][j].rearrange("(ec ep) -> ep ec", ep=128), slow=True)
        bld.dma(rec, "gpsimd", gB, ckbs[:, :],
                env["ckb"].rearrange("(ec ep) -> ep ec", ep=128), slow=True)
        gB_total = bld.c[gB]

        bld.run(rec, "vector", lambda E:
                E.memset(kcm[0].rearrange("p (ec t) -> p ec t", ec=ECH)[:, :, 0:2],
                         0.0))
        z0 = bld.tok(rec, "vector", "V", lambda E:
                     E.memset(kcm[1].rearrange("p (ec t) -> p ec t",
                                               ec=ECH)[:, :, 0:2], 0.0))

        xq_val = {}
        NTile = SLABP // TT
        ev_tok, kv_tok, mm_tok, xb_val, sp_val = {}, {}, {}, {}, {}
        job = 0
        for ti in range(NTile):
            par = ti % 2
            if ti >= 4 and ti % 2 == 0:
                bld.wait(rec, "sync", "T", mm_tok[ti - 3])
            qpar = (ti // 2) % 2
            if ti % 2 == 0:
                xw = min(XSP, SLABP - ti * TT)
                for hc in range(ECH):
                    vld = bld.dma(rec, "sync", gXQ[qpar],
                                  xq[qpar][:, hc * XSP:hc * XSP + xw],
                                  env["xn_sp"][ti * TT:ti * TT + xw,
                                               hc * 128:(hc + 1) * 128],
                                  transpose=True, slow=True)
                xq_val[ti // 2] = vld
            bld.wait(rec, "tensor", gXQ[qpar], xq_val[ti // 2])

            if ti == 0:
                bld.wait(rec, "tensor", gB, gB_total)
                bld.wait(rec, "scalar", "V", z0)
            if ti >= 2:
                bld.wait(rec, "scalar", "V", kv_tok[ti - 2])
            for wsb, dst, bias, off in ((wk, kcm[par], bks, 2),
                                        (wv, vcm[par], bvs, 0)):
                for ec in range(ECH):
                    bnk = job % 2
                    if job >= 2:
                        bld.wait(rec, "tensor", "A", ev_tok[job - 2])

                    qoff = (ti % 2) * TT

                    def p2mm(E, b, w, hc, qpar, ec, qoff=qoff):
                        return E.matmul(
                            bank(b)[:, :TT],
                            w[:, hc * D + ec * 128:hc * D + ec * 128 + 128],
                            xq[qpar][:, hc * XSP + qoff:hc * XSP + qoff + TT],
                            start=(hc == 0), stop=(hc == ECH - 1))
                    for hc in range(ECH - 1):
                        bld.run(rec, "tensor",
                                lambda E, b=bnk, w=wsb, hc=hc, qpar=qpar, ec=ec,
                                f=p2mm: f(E, b, w, hc, qpar, ec))
                    mt = bld.tok(rec, "tensor", "T",
                                 lambda E, b=bnk, w=wsb, qpar=qpar, ec=ec,
                                 f=p2mm: f(E, b, w, ECH - 1, qpar, ec))
                    bld.wait(rec, "scalar", "T", mt)
                    TWW = TW if dst is kcm[par] else TT
                    ev_tok[job] = bld.tok(
                        rec, "scalar", "A",
                        lambda E, b=bnk, dst=dst, ec=ec, off=off, TWW=TWW, bias=bias:
                        E.activation(dst[:, ec * TWW + off:ec * TWW + off + TT],
                                     bank(b)[:, :TT], AF.Identity,
                                     bias=bias[:, ec:ec + 1], scale=1.0))
                    job += 1
            mm_tok[ti] = bld.c["T"]
            bld.wait(rec, "vector", "A", ev_tok[job - 1])
            if ti >= 1:
                bld.wait(rec, "vector", xKVT[0][(ti - 1) % 2], xb_val[ti - 1][0])
                bld.wait(rec, "vector", xKVT[1][(ti - 1) % 2], xb_val[ti - 1][1])
            for ec in range(ECH):
                base = ec * TW
                bld.run(rec, "vector", lambda E, base=base, ec=ec, par=par:
                        E.tensor_scalar(cvt[:, :TT], kcm[par][:, base:base + TT],
                                        cks[:, ec:ec + 1],
                                        ckbs[:, ec:ec + 1], OP.mult, OP.add))
                bld.run(rec, "vector", lambda E, base=base, ec=ec, par=par:
                        E.scalar_tensor_tensor(cvt2[:, :TT],
                                               kcm[par][:, base + 1:base + 1 + TT],
                                               cks[:, ECH + ec:ECH + ec + 1],
                                               cvt[:, :TT], OP.mult, OP.add))
                bld.run(rec, "vector", lambda E, base=base, ec=ec, par=par:
                        E.scalar_tensor_tensor(cvt[:, :TT],
                                               kcm[par][:, base + 2:base + 2 + TT],
                                               cks[:, 2 * ECH + ec:2 * ECH + ec + 1],
                                               cvt2[:, :TT], OP.mult, OP.add))
                bld.run(rec, "vector", lambda E, ec=ec, par=par:
                        E.tensor_mul(kvo[:, ec * TT:(ec + 1) * TT],
                                     cvt[:, :TT],
                                     vcm[par][:, ec * TT:(ec + 1) * TT]))
            kv_tok[ti] = bld.tok(rec, "vector", "V", lambda E, par=par:
                                 E.tensor_copy(
                                     kcm[1 - par].rearrange(
                                         "p (ec t) -> p ec t", ec=ECH)[:, :, 0:2],
                                     kcm[par].rearrange(
                                         "p (ec t) -> p ec t",
                                         ec=ECH)[:, :, TT:TT + 2]))
            bld.wait(rec, "gpsimd", "V", kv_tok[ti])
            poff = g.F - g.NOWN - g.KLEN - 1
            r0 = ti * TT
            c0, c1 = max(r0, 2), min(r0 + TT, g.SLAB)
            if c1 > c0:
                for ec in range(ECH):
                    qi = ec % 2
                    bld.dma(rec, "gpsimd", xKVT[qi][par],
                            env["kv_sp"][ec * 128:(ec + 1) * 128,
                                         c0 + poff:c1 + poff],
                            kvo[:, ec * TT + (c0 - r0):ec * TT + (c1 - r0)],
                            slow=(c1 - c0 < 16))
            xb_val[ti] = (bld.c[xKVT[0][par]], bld.c[xKVT[1][par]])
        bld.drain(rec)
        rec.replay(nc)


def _phase2b(nc, bld, g, env):
    rec = Rec()
    bld.begin_phase()
    D, ECH, QW, NOWN = g.D, g.ECH, g.QW, g.NOWN
    bank = env["bank"]
    gB = bld.dsem("p2bgB", "gpsimd")
    xQ = bld.dsem("p2bxQ", "sync")
    xQ2 = bld.dsem("p2bxQ2", "sync")
    gWC = [bld.dsem(f"p2bgWC{i}", "gpsimd") for i in range(2)]
    xWC = [bld.dsem(f"p2bxWC{i}", "sync") for i in range(2)]

    with ExitStack() as P:
        sb = lambda name, shape, dt=BF16, **kw: P.enter_context(
            nc.sbuf_tensor(name, list(shape), dt, **kw))
        wq = sb("p2b_wq", (128, ECH * D))
        QWT = ((QW + 15) // 16) * 16
        xq = P.enter_context(nc.sbuf_tensor(
            "p2b_xq", [128, ECH * QWT], BF16, is_transpose_dest=True))
        qpre = sb("p2b_qpre", (128, ECH * QW))
        qcv = sb("p2b_qcv", (128, ECH * NOWN))
        cvt = sb("p2b_cvt", (128, 512), F32)
        cvt2 = sb("p2b_cvt2", (128, 512), F32)
        bqs = sb("p2b_bq", (128, ECH), F32)
        cqs = sb("p2b_cq", (128, 3 * ECH), F32)
        cqbs = sb("p2b_cqb", (128, ECH), F32)
        wcb = [sb(f"p2b_wc{i}", (128, 1024)) for i in range(2)]

        CW2 = min(1024, D)
        w2jobs = [(r, cc) for r in range(g.DFF // 128) for cc in range(D // CW2)]
        w2state = {"n": 0, "val": {0: 0, 1: 0}}

        def emit_w2(n):
            for _ in range(n):
                if not w2jobs:
                    return
                r, cc = w2jobs.pop(0)
                wpar = len(w2jobs) % 2
                if w2state["n"] >= 2:
                    bld.wait(rec, "gpsimd", xWC[wpar], w2state["val"][wpar])
                v = bld.dma(rec, "gpsimd", gWC[wpar], wcb[wpar][:, :CW2],
                            env["w2T"][r * 128:(r + 1) * 128,
                                       cc * CW2:(cc + 1) * CW2])
                bld.wait(rec, "sync", gWC[wpar], v)
                bld.dma(rec, "sync", xWC[wpar],
                        env["w2_bf"][r * 128:(r + 1) * 128,
                                     cc * CW2:(cc + 1) * CW2],
                        wcb[wpar][:, :CW2])
                w2state["val"][wpar] = bld.c[xWC[wpar]]
                w2state["n"] += 1

        bld.dma(rec, "gpsimd", gB, wq.rearrange("p (hc e) -> p hc e", hc=ECH),
                env["wqT"].rearrange("(hc hp) e -> hp hc e", hp=128))
        bld.dma(rec, "gpsimd", gB, bqs[:, :],
                env["bq"].rearrange("(ec ep) -> ep ec", ep=128), slow=True)
        for j in range(3):
            bld.dma(rec, "gpsimd", gB, cqs[:, j * ECH:(j + 1) * ECH],
                    env["cqw"][j].rearrange("(ec ep) -> ep ec", ep=128), slow=True)
        bld.dma(rec, "gpsimd", gB, cqbs[:, :],
                env["cqb"].rearrange("(ec ep) -> ep ec", ep=128), slow=True)
        c0 = g.R0 - 2
        QWT = ((QW + 15) // 16) * 16
        for hc in range(ECH):
            bld.dma(rec, "sync", xQ2, xq[:, hc * QWT:hc * QWT + QWT],
                    env["xn_sp"][c0:c0 + QWT, hc * 128:(hc + 1) * 128],
                    transpose=True, slow=True)
        bld.wait(rec, "tensor", gB, bld.c[gB])
        bld.wait(rec, "tensor", xQ2, bld.c[xQ2])

        ev_tok = {}
        job = 0
        CQ = min(512, NOWN)
        QWT = ((QW + 15) // 16) * 16
        for ec in range(ECH):
            co = 0
            for w in g.QCHUNKS:
                bnk = job % 2
                if job >= 2:
                    bld.wait(rec, "tensor", "A", ev_tok[job - 2])

                def p2bmm(E, b, hc, ec, co, w):
                    return E.matmul(
                        bank(b)[:, :w],
                        wq[:, hc * D + ec * 128:hc * D + ec * 128 + 128],
                        xq[:, hc * QWT + co:hc * QWT + co + w],
                        start=(hc == 0), stop=(hc == ECH - 1))
                for hc in range(ECH - 1):
                    bld.run(rec, "tensor",
                            lambda E, b=bnk, hc=hc, ec=ec, co=co, w=w,
                            f=p2bmm: f(E, b, hc, ec, co, w))
                mt = bld.tok(rec, "tensor", "T",
                             lambda E, b=bnk, ec=ec, co=co, w=w, f=p2bmm:
                             f(E, b, ECH - 1, ec, co, w))
                bld.wait(rec, "scalar", "T", mt)
                ev_tok[job] = bld.tok(
                    rec, "scalar", "A",
                    lambda E, b=bnk, ec=ec, co=co, w=w:
                    E.activation(qpre[:, ec * QW + co:ec * QW + co + w],
                                 bank(b)[:, :w], AF.Identity,
                                 bias=bqs[:, ec:ec + 1], scale=1.0))
                co += w
                job += 1
            bld.wait(rec, "vector", "A", ev_tok[job - 1])
            base = ec * QW
            qv = None
            for jt in range(NOWN // CQ):
                o = jt * CQ
                bld.run(rec, "vector", lambda E, base=base, ec=ec, o=o:
                        E.tensor_scalar(cvt[:, :CQ],
                                        qpre[:, base + o:base + o + CQ],
                                        cqs[:, ec:ec + 1],
                                        cqbs[:, ec:ec + 1], OP.mult, OP.add))
                bld.run(rec, "vector", lambda E, base=base, ec=ec, o=o:
                        E.scalar_tensor_tensor(cvt2[:, :CQ],
                                               qpre[:, base + o + 1:base + o + 1 + CQ],
                                               cqs[:, ECH + ec:ECH + ec + 1],
                                               cvt[:, :CQ], OP.mult, OP.add))
                qv = bld.tok(rec, "vector", "V",
                             lambda E, base=base, ec=ec, o=o:
                             E.scalar_tensor_tensor(
                                 qcv[:, ec * NOWN + o:ec * NOWN + o + CQ],
                                 qpre[:, base + o + 2:base + o + 2 + CQ],
                                 cqs[:, 2 * ECH + ec:2 * ECH + ec + 1],
                                 cvt2[:, :CQ], OP.mult, OP.add))
            bld.wait(rec, "sync", "V", qv)
            for tc in range(g.TCHO):
                bld.dma(rec, "sync", xQ,
                        env["qtm"][:, tc * D + ec * 128:tc * D + ec * 128 + 128],
                        qcv[:, ec * NOWN + tc * 128:ec * NOWN + tc * 128 + 128],
                        transpose=True)
            emit_w2(2)
        emit_w2(len(w2jobs))
        bld.drain(rec)
        rec.replay(nc)


def _phase3(nc, bld, g, env):
    rec = Rec()
    bld.begin_phase()
    D, F, KC, PCH, GSZ, GR, NOWN, TCHO = g.D, g.F, g.KC, g.PCH, g.GSZ, g.GR, \
        g.NOWN, g.TCHO
    bank = env["bank"]
    PCO = PCH - TCHO
    gKV = [bld.dsem(f"p3gKV{i}", "sync") for i in range(2)]
    TFD = 2
    gTF = [bld.dsem(f"p3gTF{i}", "gpsimd") for i in range(TFD)]
    gKH = [bld.dsem(f"p3gKH{i}", "gpsimd") for i in range(2)]
    TID = 4
    gTI = [bld.dsem(f"p3gTI{i}", "gpsimd") for i in range(TID)]
    gWC = [bld.dsem(f"p3gWC{i}", "gpsimd") for i in range(2)]
    xWC = [bld.dsem(f"p3xWC{i}", "sync") for i in range(2)]
    gDR = bld.dsem("p3gDR", "gpsimd")

    with ExitStack() as P:
        sb = lambda name, shape, dt=BF16: P.enter_context(
            nc.sbuf_tensor(name, list(shape), dt))
        kvg = [P.enter_context(nc.sbuf_tensor(
            f"p3_kvg{i}", [128, PCH * GSZ], BF16, is_transpose_dest=True))
            for i in range(2)]
        tf = [sb(f"p3_tf{i}", (128, PCH * 128)) for i in range(TFD)]
        X = [sb("p3_xr", (128, KC * GSZ)), sb("p3_xi", (128, KC * GSZ))]
        Z = X
        Kh = [[sb(f"p3_kr{i}", (128, GSZ)), sb(f"p3_ki{i}", (128, GSZ))]
              for i in range(2)]
        tmp = sb("p3_tmp", (128, GSZ))
        tmp2 = sb("p3_tmp2", (128, GSZ))
        tmp3 = sb("p3_tmp3", (128, GSZ))
        tmp4 = sb("p3_tmp4", (128, GSZ))
        ti = [sb(f"p3_ti{i}", (128, NOWN)) for i in range(TID)]
        t1 = sb("p3_t1", (128, GSZ))
        t2 = sb("p3_t2", (128, GSZ), F32)
        wcb = [sb(f"p3_wc{i}", (128, 1024)) for i in range(2)]

        bld.dma(rec, "gpsimd", gDR, env["drep"][:, :], env["dskip_rep"][:, :])
        bld.wait(rec, "vector", gDR, 16)
        wjobs = []
        for name, dst, rows, cols in (("w1_lay", "w1_bf", g.DFF, D),):
            CW = min(1024, cols)
            for r in range(rows // 128):
                for cc in range(cols // CW):
                    wjobs.append((name, dst, r, cc, CW))
        wj_seen = [0]
        wc_val = {0: 0, 1: 0}

        def emit_wjobs(n):
            for _ in range(n):
                if not wjobs:
                    return
                name, dst, r, cc, CW = wjobs.pop(0)
                wpar = len(wjobs) % 2
                if wj_seen[0] >= 2:
                    bld.wait(rec, "gpsimd", xWC[wpar], wc_val[wpar])
                v = bld.dma(rec, "gpsimd", gWC[wpar], wcb[wpar][:, :CW],
                            env[name][r * 128:(r + 1) * 128,
                                      cc * CW:(cc + 1) * CW])
                bld.wait(rec, "sync", gWC[wpar], v)
                bld.dma(rec, "sync", xWC[wpar],
                        env[dst][r * 128:(r + 1) * 128, cc * CW:(cc + 1) * CW],
                        wcb[wpar][:, :CW])
                wc_val[wpar] = bld.c[xWC[wpar]]
                wj_seen[0] += 1

        ev_tok, ld_tok, inv_mm = {}, {}, {}
        kv_need = {}
        y2v_hist = {}
        y2v = None
        for gg in range(GR):
            kp3 = gg % 2
            if gg == 0:
                for pc in range(PCH):
                    bld.dma(rec, "sync", gKV[0],
                            kvg[0][:, pc * GSZ:(pc + 1) * GSZ],
                            env["kv_sp"][0:GSZ, pc * 128:(pc + 1) * 128],
                            transpose=True, slow=True)
                kv_need[0] = bld.c[gKV[0]]
            bld.wait(rec, "tensor", gKV[kp3], kv_need[gg])
            job = 0
            for kc in range(KC):
                for ri in range(2):
                    gjob = gg * 2 * KC + job
                    par = gjob % TFD
                    eng = "gpsimd"
                    if gjob - TFD >= 0:
                        bld.wait(rec, eng, "T", ld_tok[gjob - TFD])
                    v = bld.dma(rec, eng, gTF[par],
                                tf[par][:, :], env["tblf_sp"][ri, kc])
                    emit_wjobs(1)
                    bld.wait(rec, "tensor", gTF[par], v)
                    bnk = job % 2
                    if job >= 2:
                        bld.wait(rec, "tensor", "V", ev_tok[(gg, job - 2)])

                    def p3mm(E, b, par, pc, kp3=kp3):
                        return E.matmul(bank(b)[:, :GSZ],
                                        tf[par][:, pc * 128:(pc + 1) * 128],
                                        kvg[kp3][:, pc * GSZ:(pc + 1) * GSZ],
                                        start=(pc == g.PC0), stop=(pc == PCH - 1))
                    for pc in range(g.PC0, PCH - 1):
                        bld.run(rec, "tensor",
                                lambda E, b=bnk, par=par, pc=pc, f=p3mm:
                                f(E, b, par, pc))
                    mt = bld.tok(rec, "tensor", "T",
                                 lambda E, b=bnk, par=par, f=p3mm:
                                 f(E, b, par, PCH - 1))
                    ld_tok[gjob] = mt
                    bld.wait(rec, "vector", "T", mt)
                    ev_tok[(gg, job)] = bld.tok(
                        rec, "vector", "V",
                        lambda E, b=bnk, ri=ri, kc=kc:
                        E.tensor_copy(X[ri][:, kc * GSZ:(kc + 1) * GSZ],
                                      bank(b)[:, :GSZ]))
                    job += 1
            if gg + 1 < GR:
                if gg >= 1:
                    bld.wait(rec, "sync", "V", y2v_hist[gg - 1])
                for pc in range(PCH):
                    bld.dma(rec, "sync", gKV[(gg + 1) % 2],
                            kvg[(gg + 1) % 2][:, pc * GSZ:(pc + 1) * GSZ],
                            env["kv_sp"][(gg + 1) * GSZ:(gg + 2) * GSZ,
                                         pc * 128:(pc + 1) * 128],
                            transpose=True, slow=True)
                kv_need[gg + 1] = bld.c[gKV[(gg + 1) % 2]]
            pw_tok = None
            pw_done = {}
            for kc in range(KC):
                kp = kc % 2
                if kc >= 2:
                    bld.wait(rec, "gpsimd", "V", pw_done[kc - 2])
                elif gg >= 1:
                    last_same = KC - 1 - ((KC - 1 - kc) % 2)
                    bld.wait(rec, "gpsimd", "V", pw_prev[last_same])
                for ri in range(2):
                    bld.dma(rec, "gpsimd", gKH[kp],
                            Kh[kp][ri][:, :], env["khat_sp"][ri, kc, gg])
                bld.wait(rec, "vector", gKH[kp], bld.c[gKH[kp]])
                s = slice(kc * GSZ, (kc + 1) * GSZ)
                bld.run(rec, "vector", lambda E, s=s, kp=kp:
                        E.tensor_mul(tmp[:, :], X[0][:, s], Kh[kp][0][:, :]))
                bld.run(rec, "vector", lambda E, s=s, kp=kp:
                        E.tensor_mul(tmp2[:, :], X[1][:, s], Kh[kp][1][:, :]))
                bld.run(rec, "vector", lambda E, s=s, kp=kp:
                        E.tensor_mul(tmp3[:, :], X[0][:, s], Kh[kp][1][:, :]))
                bld.run(rec, "vector", lambda E, s=s, kp=kp:
                        E.tensor_mul(tmp4[:, :], X[1][:, s], Kh[kp][0][:, :]))
                bld.run(rec, "vector", lambda E, s=s:
                        E.tensor_sub(Z[0][:, s], tmp[:, :], tmp2[:, :]))
                pw_tok = bld.tok(rec, "vector", "V", lambda E, s=s:
                                 E.tensor_add(Z[1][:, s], tmp3[:, :], tmp4[:, :]))
                pw_done[kc] = pw_tok
            pw_prev = pw_done
            bld.wait(rec, "tensor", "V", pw_tok)
            first = True
            for kc in range(KC):
                for ri in range(2):
                    idx = kc * 2 + ri
                    gidx = gg * 2 * KC + idx
                    par = gidx % TID
                    eng = "gpsimd"
                    if gidx - TID >= 0:
                        bld.wait(rec, eng, "T", inv_mm[gidx - TID])
                    v = bld.dma(rec, eng, gTI[par], ti[par][:, :],
                                env["tbli_sp"][ri, kc * 128:(kc + 1) * 128, :])
                    bld.wait(rec, "tensor", gTI[par], v)
                    last = (kc == KC - 1) and (ri == 1)

                    def pinvmm(E, tc, par=par, ri=ri, kc=kc, first=first,
                               last=last):
                        return E.matmul(bank(tc)[:, :GSZ],
                                        ti[par][:, tc * 128:(tc + 1) * 128],
                                        Z[ri][:, kc * GSZ:(kc + 1) * GSZ],
                                        start=first, stop=last)
                    for tc in range(TCHO - 1):
                        bld.run(rec, "tensor", lambda E, tc=tc, f=pinvmm: f(E, tc))
                    inv_mm[gidx] = bld.tok(
                        rec, "tensor", "T", lambda E, f=pinvmm: f(E, TCHO - 1))
                    first = False
            mtv = bld.c["T"]
            bld.wait(rec, "vector", "T", mtv)
            for tc in range(TCHO):
                kvs = slice((PCO + tc) * GSZ, (PCO + tc + 1) * GSZ)
                bld.run(rec, "vector", lambda E, kvs=kvs, gg=gg, kp3=kp3:
                        E.tensor_mul(t1[:, :], kvg[kp3][:, kvs],
                                     env["drep"][:, gg * GSZ:(gg + 1) * GSZ]))
                bld.run(rec, "vector", lambda E, tc=tc:
                        E.tensor_add(t2[:, :], bank(tc)[:, :GSZ], t1[:, :]))
                qs = slice(tc * D + gg * GSZ, tc * D + (gg + 1) * GSZ)

                def y2mul(E, qs=qs):
                    return E.tensor_mul(env["y2tm"][:, qs], t2[:, :],
                                        env["qtm"][:, qs])
                if tc == TCHO - 1:
                    y2v = bld.tok(rec, "vector", "V", y2mul)
                else:
                    bld.run(rec, "vector", y2mul)
            y2v_hist[gg] = y2v
            y2v_hist[gg] = y2v
            bld.wait(rec, "tensor", "V", y2v)
        emit_wjobs(len(wjobs))
        bld.drain(rec)
        rec.replay(nc)


def _phase4(nc, bld, g, env):
    rec = Rec()
    bld.begin_phase()
    D, ECH, NOWN, TCHO = g.D, g.ECH, g.NOWN, g.TCHO
    bank = env["bank"]
    EOSZ = min(512, D)
    EON = D // EOSZ
    gWO = bld.dsem("p4gWO", "gpsimd")
    xY = bld.dsem("p4xY", "sync")
    xAB = [bld.dsem(f"p4xAB{i}", "sync") for i in range(2)]

    with ExitStack() as P:
        sb = lambda name, shape, dt=BF16, **kw: P.enter_context(
            nc.sbuf_tensor(name, list(shape), dt, **kw))
        y2fm = sb("p4_y2fm", (128, ECH * NOWN), is_transpose_dest=True)
        wo = sb("p4_wo", (128, ECH * D))
        ab = [sb(f"p4_ab{i}", (128, EOSZ)) for i in range(2)]

        bld.dma(rec, "gpsimd", gWO, wo.rearrange("p (hc e) -> p hc e", hc=ECH),
                env["woT"].rearrange("(hc hp) e -> hp hc e", hp=128))
        for tc in range(TCHO):
            for ec in range(ECH):
                bld.dma(rec, "sync", xY,
                        y2fm[:, ec * NOWN + tc * 128:ec * NOWN + tc * 128 + 128],
                        env["y2tm"][:, tc * D + ec * 128:tc * D + ec * 128 + 128],
                        transpose=True)
        bld.wait(rec, "tensor", xY, bld.c[xY])
        bld.wait(rec, "tensor", gWO, bld.c[gWO])
        job = 0
        ev_tok = {}
        for tc in range(TCHO):
            for eo in range(EON):
                bnk = job % 2
                if job >= 2:
                    bld.wait(rec, "tensor", "A", ev_tok[job - 2])

                def p4mm(E, b, hc, tc, eo):
                    return E.matmul(
                        bank(b)[:, :EOSZ],
                        y2fm[:, hc * NOWN + tc * 128:hc * NOWN + tc * 128 + 128],
                        wo[:, hc * D + eo * EOSZ:hc * D + (eo + 1) * EOSZ],
                        start=(hc == 0), stop=(hc == ECH - 1))
                for hc in range(ECH - 1):
                    bld.run(rec, "tensor", lambda E, b=bnk, hc=hc, tc=tc, eo=eo:
                            p4mm(E, b, hc, tc, eo))
                mt = bld.tok(rec, "tensor", "T", lambda E, b=bnk, tc=tc, eo=eo:
                             p4mm(E, b, ECH - 1, tc, eo))
                bld.wait(rec, "scalar", "T", mt)
                kpar = job % 2
                bld.wait(rec, "scalar", xAB[kpar], bld.c[xAB[kpar]])
                ev_tok[job] = bld.tok(rec, "scalar", "A",
                                      lambda E, b=bnk, kpar=kpar:
                                      E.copy(ab[kpar][:, :], bank(b)[:, :EOSZ]))
                bld.wait(rec, "sync", "A", ev_tok[job])
                bld.dma(rec, "sync", xAB[kpar],
                        env["attn_sp"][tc * 128:(tc + 1) * 128,
                                       eo * EOSZ:(eo + 1) * EOSZ],
                        ab[kpar][:, :])
                job += 1
        bld.drain(rec)
        rec.replay(nc)


def _phase5(nc, bld, g, env):
    rec = Rec()
    bld.begin_phase()
    D, DFF, ECH, FCH, NOWN, TCHO = g.D, g.DFF, g.ECH, g.FCH, g.NOWN, g.TCHO
    bank = env["bank"]
    TJ = min(512, NOWN)
    NTJ = NOWN // TJ
    EOSZ = min(512, D)
    EON = D // EOSZ
    gB = bld.dsem("p5gB", "gpsimd")
    xXN = bld.dsem("p5xXN", "sync")
    gW1 = [bld.dsem(f"p5gW1{i}", ("gpsimd", "sync")[i % 2]) for i in range(2)]
    gW2 = [bld.dsem(f"p5gW2{i}", ("gpsimd", "sync")[i % 2]) for i in range(2)]
    gXR = [bld.dsem(f"p5gXR{i}", "gpsimd") for i in range(2)]
    xO = [bld.dsem(f"p5xO{i}", "sync") for i in range(2)]

    with ExitStack() as P:
        sb = lambda name, shape, dt=BF16, **kw: P.enter_context(
            nc.sbuf_tensor(name, list(shape), dt, **kw))
        xn = P.enter_context(nc.sbuf_tensor(
            "p5_xn", [128, ECH * NOWN], BF16, is_transpose_dest=True))
        a = sb("p5_a", (128, FCH * NOWN))
        w1c = [sb(f"p5_w1{i}", (128, ECH * 128)) for i in range(2)]
        w2c = [sb(f"p5_w2{i}", (128, EOSZ)) for i in range(2)]
        ub = sb("p5_u", (128, TJ), F32)
        iib = sb("p5_ii", (128, TJ), mybir.dt.int32)
        mb = sb("p5_m", (128, TJ), F32)
        fb = sb("p5_f", (128, TJ), F32)
        cb = sb("p5_c", (128, TJ), F32)
        zb = sb("p5_z", (128, TJ), F32)
        xr = [sb(f"p5_xr{i}", (128, EOSZ), F32) for i in range(2)]
        ar = [sb(f"p5_ar{i}", (128, EOSZ)) for i in range(2)]
        ob = [sb(f"p5_ob{i}", (128, EOSZ), F32) for i in range(2)]
        brp = sb("p5_bo", (128, D), F32)
        t3 = sb("p5_t3", (128, EOSZ), F32)

        bld.dma(rec, "gpsimd", gB, brp[:, :], env["bo_rep"][:, :])
        for hc in range(ECH):
            bld.dma(rec, "sync", xXN, xn[:, hc * NOWN:(hc + 1) * NOWN],
                    env["xn_sp"][g.R0:g.R0 + NOWN, hc * 128:(hc + 1) * 128],
                    transpose=True, slow=True)
        bld.wait(rec, "tensor", gB, bld.c[gB])
        bld.wait(rec, "tensor", xXN, bld.c[xXN])
        job = 0
        ev_tok, w1_tok = {}, {}
        ga = None
        for fc in range(FCH):
            par = fc % 2
            if fc >= 2:
                bld.wait(rec, ("gpsimd", "sync")[par % 2], "T", w1_tok[fc - 2])
            v = bld.dma(rec, ("gpsimd", "sync")[par % 2], gW1[par],
                        w1c[par][:, :],
                        env["w1_bf"][fc * 128:(fc + 1) * 128, :])
            bld.wait(rec, "tensor", gW1[par], v)
            for tj in range(NTJ):
                bnk = job % 2
                if job >= 2:
                    bld.wait(rec, "tensor", "V", ev_tok[job - 2])

                def p5mm(E, b, par, hc, tj):
                    return E.matmul(
                        bank(b)[:, :TJ],
                        w1c[par][:, hc * 128:(hc + 1) * 128],
                        xn[:, hc * NOWN + tj * TJ:hc * NOWN + (tj + 1) * TJ],
                        start=(hc == 0), stop=(hc == ECH - 1))
                for hc in range(ECH - 1):
                    bld.run(rec, "tensor", lambda E, b=bnk, par=par, hc=hc, tj=tj:
                            p5mm(E, b, par, hc, tj))
                mt = bld.tok(rec, "tensor", "T", lambda E, b=bnk, par=par, tj=tj:
                             p5mm(E, b, par, ECH - 1, tj))
                bld.wait(rec, "vector", "T", mt)
                if ga is not None:
                    bld.wait(rec, "vector", "A", ga)
                ev_tok[job] = bld.tok(rec, "vector", "V", lambda E, b=bnk:
                        E.tensor_scalar(ub[:, :], bank(b)[:, :TJ], 10.0, 4096.5,
                                        OP.mult, OP.add))
                bld.run(rec, "vector", lambda E:
                        E.tensor_copy(iib[:, :], ub[:, :]))
                bld.run(rec, "vector", lambda E:
                        E.tensor_copy(fb[:, :], iib[:, :]))
                bld.run(rec, "vector", lambda E:
                        E.tensor_tensor(cb[:, :], fb[:, :], ub[:, :], OP.is_gt))
                bld.run(rec, "vector", lambda E:
                        E.tensor_sub(mb[:, :], fb[:, :], cb[:, :]))
                bld.run(rec, "vector", lambda E:
                        E.tensor_scalar(cb[:, :], mb[:, :], 4096.0, -32768.0,
                                        OP.subtract, OP.max))
                qv = bld.tok(rec, "vector", "V", lambda E:
                             E.tensor_scalar(zb[:, :], cb[:, :], 32767.0, 0.1,
                                             OP.min, OP.mult))
                bld.wait(rec, "scalar", "V", qv)
                ga = bld.tok(rec, "scalar", "A",
                             lambda E, fc=fc, tj=tj:
                             E.activation(a[:, fc * NOWN + tj * TJ:
                                            fc * NOWN + (tj + 1) * TJ],
                                          zb[:, :], AF.Gelu_apprx_tanh))
                job += 1
            w1_tok[fc] = bld.c["T"]
        bld.wait(rec, "tensor", "A", bld.c["A"])
        ev2, w2_tok, st2 = {}, {}, {}
        for eo in range(EON):
            for fc in range(FCH):
                par = fc % 2
                weng = ("gpsimd", "sync")[par % 2]
                if fc >= 2:
                    bld.wait(rec, weng, "T", w2_tok[(eo, fc - 2)])
                elif eo >= 1:
                    bld.wait(rec, weng, "T", w2_tok[(eo - 1, FCH - 2 + fc)])
                v = bld.dma(rec, weng, gW2[par], w2c[par][:, :],
                            env["w2_bf"][fc * 128:(fc + 1) * 128,
                                         eo * EOSZ:(eo + 1) * EOSZ])
                bld.wait(rec, "tensor", gW2[par], v)
                if fc == 0 and eo >= 1:
                    for tc in range(TCHO):
                        bld.wait(rec, "tensor", "V", ev2[(eo - 1, tc)])

                def w2mm(E, tc, par=par, fc=fc):
                    return E.matmul(bank(tc)[:, :EOSZ],
                                    a[:, fc * NOWN + tc * 128:
                                      fc * NOWN + tc * 128 + 128],
                                    w2c[par][:, :],
                                    start=(fc == 0), stop=(fc == FCH - 1))
                for tc in range(TCHO - 1):
                    bld.run(rec, "tensor", lambda E, tc=tc, f=w2mm: f(E, tc))
                w2_tok[(eo, fc)] = bld.tok(rec, "tensor", "T",
                                           lambda E, f=w2mm: f(E, TCHO - 1))
            mmt = bld.c["T"]
            for tc in range(TCHO):
                par = tc % 2
                r0 = g.R0 + tc * 128
                if tc >= 2:
                    bld.wait(rec, "gpsimd", "V", ev2[(eo, tc - 2)])
                elif eo >= 1:
                    bld.wait(rec, "gpsimd", "V", ev2[(eo - 1, TCHO - 2 + tc)])
                vx2 = bld.dma(rec, "gpsimd", gXR[par], xr[par][:, :],
                              env["x_slab"][r0:r0 + 128,
                                            eo * EOSZ:(eo + 1) * EOSZ])
                va2 = bld.dma(rec, "gpsimd", gXR[par], ar[par][:, :],
                              env["attn_sp"][tc * 128:(tc + 1) * 128,
                                             eo * EOSZ:(eo + 1) * EOSZ])
                bld.wait(rec, "vector", "T", mmt)
                bld.wait(rec, "vector", gXR[par], va2)
                if tc >= 2:
                    bld.wait(rec, "vector", xO[par], bld.c[xO[par]])
                elif eo >= 1:
                    bld.wait(rec, "vector", xO[par], bld.c[xO[par]])
                bld.run(rec, "vector", lambda E, tc=tc, par=par:
                        E.tensor_add(t3[:, :], bank(tc)[:, :EOSZ], xr[par][:, :]))
                bld.run(rec, "vector", lambda E, par=par:
                        E.tensor_add(t3[:, :], t3[:, :], ar[par][:, :]))
                ev2[(eo, tc)] = bld.tok(
                    rec, "vector", "V",
                    lambda E, par=par, eo=eo:
                    E.tensor_add(ob[par][:, :], t3[:, :],
                                 brp[:, eo * EOSZ:(eo + 1) * EOSZ]))
                bld.wait(rec, "sync", "V", ev2[(eo, tc)])
                bld.dma(rec, "sync", xO[par],
                        env["out"][tc * 128:(tc + 1) * 128,
                                   eo * EOSZ:(eo + 1) * EOSZ],
                        ob[par][:, :])
        bld.drain(rec)
        rec.replay(nc)



def kernel(**inputs):
    cfg = full_cfg()
    in_maps = prepare_in_maps(cfg, inputs)
    nc = build_nc(cfg)
    res = run_bass_kernel_spmd(nc, in_maps, core_ids=list(range(NCORES)))
    out = np.empty((cfg.B, cfg.L, cfg.D), dtype=np.float32)
    for c in range(NCORES):
        b = c // cfg.CPB
        t0 = cfg.NOWN * (c % cfg.CPB)
        out[b, t0:t0 + cfg.NOWN, :] = res.results[c]["out"]
    return out


# version 23
# speedup vs baseline: 1.1568x; 1.0690x over previous
import numpy as np
from contextlib import ExitStack

import concourse.bass as bass
import concourse.mybir as mybir
from concourse.bass_utils import run_bass_kernel_spmd

F32 = mybir.dt.float32
BF16 = mybir.dt.bfloat16
AF = mybir.ActivationFunctionType
OP = mybir.AluOpType

NCORES = 8


class CFG:
    def __init__(self, B=2, L=4096, D=2048, DFF=8192, KLEN=2048, F=4096,
                 TT=256, GSZ=512):
        self.B, self.L, self.D, self.DFF, self.KLEN, self.F = B, L, D, DFF, KLEN, F
        self.TT = TT
        self.GSZ = GSZ
        self.CPB = NCORES // B
        self.NOWN = L // self.CPB
        self.SLAB = self.NOWN + KLEN + 1
        self.SLABP = ((self.SLAB + TT - 1) // TT) * TT
        self.BINS = F // 2 + 1
        self.KC = (self.BINS + 127) // 128
        self.BINSP = 128 * self.KC
        self.PCH = F // 128
        self.p_nz = F - self.NOWN - KLEN + 1
        self.PC0 = self.p_nz // 128
        self.ECH = D // 128
        self.FCH = DFF // 128
        self.TCHO = self.NOWN // 128
        self.GR = D // GSZ
        self.QW = self.NOWN + 2
        self.QCHUNKS = [512] * (self.QW // 512) + (
            [self.QW % 512] if self.QW % 512 else [])
        self.SC = KLEN // 128
        self.R0 = KLEN + 1
        assert D % 128 == 0 and DFF % 128 == 0 and KLEN % 128 == 0
        assert F % 128 == 0 and self.NOWN % 128 == 0 and TT % 128 == 0
        assert F >= self.NOWN + KLEN - 1 and self.p_nz >= 0
        assert self.TCHO <= 8 and GSZ <= 512 and TT <= 512
        assert self.SLABP % 128 == 0 and D % GSZ == 0


def full_cfg():
    return CFG()


def small_cfg():
    return CFG(B=2, L=1024, D=256, DFF=1024, KLEN=256, F=512, TT=256, GSZ=128)



def make_tables(cfg):
    F, BINS, BINSP, NOWN = cfg.F, cfg.BINS, cfg.BINSP, cfg.NOWN
    p = np.arange(F, dtype=np.float64)[:, None]
    k = np.arange(BINSP, dtype=np.float64)[None, :]
    ang = 2.0 * np.pi * p * k / F
    fwd_cos = np.cos(ang)
    fwd_sin = -np.sin(ang)
    fwd_cos[:, BINS:] = 0.0
    fwd_sin[:, BINS:] = 0.0
    kk = np.arange(BINSP, dtype=np.float64)[:, None]
    t = np.arange(NOWN, dtype=np.float64)[None, :]
    ck = np.full((BINSP, 1), 2.0 / F)
    ck[0, 0] = 1.0 / F
    if F // 2 < BINSP:
        ck[F // 2, 0] = 1.0 / F
    ck[BINS:, 0] = 0.0
    ang2 = 2.0 * np.pi * (F - NOWN + t) * kk / F
    inv_cos = ck * np.cos(ang2)
    inv_sin = -ck * np.sin(ang2)
    def lay(t):
        KC, PCH = cfg.KC, cfg.PCH
        return np.ascontiguousarray(
            t.reshape(PCH, 128, KC, 128).transpose(2, 1, 0, 3)
            .reshape(KC, 128, PCH * 128).astype(np.float32))
    return {
        "fwd_cos": lay(fwd_cos),
        "fwd_sin": lay(fwd_sin),
        "inv_cos": inv_cos.astype(np.float32),
        "inv_sin": inv_sin.astype(np.float32),
    }


def prepare_in_maps(cfg, inputs):
    x = np.asarray(inputs["L0_attn_u"], dtype=np.float32)
    tabs = make_tables(cfg)
    com = {
        "wqT": np.ascontiguousarray(np.asarray(inputs["L0_attn_Wq"], np.float32).T),
        "wkT": np.ascontiguousarray(np.asarray(inputs["L0_attn_Wk"], np.float32).T),
        "wvT": np.ascontiguousarray(np.asarray(inputs["L0_attn_Wv"], np.float32).T),
        "woT": np.ascontiguousarray(np.asarray(inputs["L0_attn_Wo"], np.float32).T),
        "w1T": np.ascontiguousarray(np.asarray(inputs["L0_ffn_W1"], np.float32).T),
        "w2T": np.ascontiguousarray(np.asarray(inputs["L0_ffn_W2"], np.float32).T),
        "w1_lay": None,
        "bq": np.asarray(inputs["L0_attn_bq"], np.float32),
        "bk": np.asarray(inputs["L0_attn_bk"], np.float32),
        "bv": np.asarray(inputs["L0_attn_bv"], np.float32),
        "bo_rep": np.ascontiguousarray(
            np.broadcast_to(np.asarray(inputs["L0_attn_bo"], np.float32)[None, :],
                            (128, cfg.D))),
        "cqw": np.ascontiguousarray(
            np.asarray(inputs["L0_attn_convq_w"], np.float32)[:, 0, :].T),
        "ckw": np.ascontiguousarray(
            np.asarray(inputs["L0_attn_convk_w"], np.float32)[:, 0, :].T),
        "cqb": np.asarray(inputs["L0_attn_convq_b"], np.float32),
        "ckb": np.asarray(inputs["L0_attn_convk_b"], np.float32),
        "dskip_rep": np.ascontiguousarray(
            np.broadcast_to(np.asarray(inputs["L0_attn_D_skip"], np.float32)[None, :],
                            (128, cfg.D))),
        "ssmT": np.ascontiguousarray(
            np.asarray(inputs["L0_attn_ssm_kernel"], np.float32).T),
    }
    com["w1_lay"] = np.ascontiguousarray(
        com["w1T"].reshape(cfg.ECH, 128, cfg.FCH, 128)
        .transpose(2, 1, 0, 3).reshape(cfg.DFF, cfg.D))
    del com["w1T"]
    com.update(tabs)
    in_maps = []
    for c in range(NCORES):
        b = c // cfg.CPB
        t0 = cfg.NOWN * (c % cfg.CPB)
        slab = np.zeros((cfg.SLABP, cfg.D), dtype=np.float32)
        lo = t0 - (cfg.KLEN + 1)
        src_lo = max(lo, 0)
        slab[src_lo - lo:(t0 + cfg.NOWN) - lo, :] = x[b, src_lo:t0 + cfg.NOWN, :]
        m = dict(com)
        m["x_slab"] = slab
        in_maps.append(m)
    return in_maps



ENGS = ("sync", "gpsimd", "tensor", "vector", "scalar")


class Rec:
    def __init__(self):
        self.q = {e: [] for e in ENGS}

    def add(self, eng, fn):
        self.q[eng].append(fn)

    def replay(self, nc):
        with nc.Block() as blk:
            for e in ENGS:
                ops = self.q[e]
                if not ops:
                    continue

                def run(proxy, ops=ops):
                    for fn in ops:
                        fn(proxy)
                getattr(blk, e)(run)


class Builder:
    def __init__(self, nc, S):
        self.nc = nc
        self.S = S
        self.c = {}
        self.sems = {}
        self.phase_sems = []
        for name in ("T", "V", "A"):
            self._mk(name)

    def _mk(self, name):
        self.sems[name] = self.S.enter_context(self.nc.semaphore(f"s_{name}"))
        self.c[name] = 0
        return name

    def begin_phase(self):
        self.phase_sems = []

    def dsem(self, name, eng):
        if name not in self.sems:
            self._mk(name)
        self.phase_sems.append((name, eng))
        return name

    def drain(self, rec):
        for name, eng in self.phase_sems:
            if self.c[name]:
                self.wait(rec, eng, name, self.c[name])

    def dma(self, rec, eng, sem, out, in_, transpose=False, slow=False):
        self.c[sem] += 16
        v = self.c[sem]
        h = self.sems[sem]

        def fn(E, out=out, in_=in_, h=h, tr=transpose, slow=slow):
            E.dma_start(out=out, in_=in_, transpose=tr,
                        allow_slow_non_contiguous=slow).then_inc(h, 16)
        rec.add(eng, fn)
        return v

    def wait(self, rec, eng, sem, v):
        h = self.sems[sem]
        rec.add(eng, lambda E, h=h, v=v: E.wait_ge(h, v))

    def tok(self, rec, eng, sem, mk):
        prev = self.c[sem]
        self.c[sem] += 1
        v = self.c[sem]
        h = self.sems[sem]
        if prev:
            rec.add(eng, lambda E, h=h, prev=prev: E.wait_ge(h, prev))
        rec.add(eng, lambda E, mk=mk, h=h: mk(E).then_inc(h, 1))
        return v

    _ENG_SEM = {"vector": "V", "scalar": "A"}

    def run(self, rec, eng, mk):
        sem = self._ENG_SEM.get(eng)
        if sem is not None:
            self.tok(rec, eng, sem, mk)
        else:
            rec.add(eng, mk)



def build_nc(cfg):
    nc = bass.Bass("TRN2", target_bir_lowering=False)
    g = cfg
    D, F = g.D, g.F

    def din(name, shape, dt=F32):
        return nc.dram_tensor(name, list(shape), dt, kind="ExternalInput").ap()

    env = {}
    env["x_slab"] = din("x_slab", (g.SLABP, D))
    for n in ("wqT", "wkT", "wvT", "woT"):
        env[n] = din(n, (D, D))
    env["w1_lay"] = din("w1_lay", (g.DFF, D))
    env["w2T"] = din("w2T", (g.DFF, D))
    for n in ("bq", "bk", "bv", "cqb", "ckb"):
        env[n] = din(n, (D,))
    env["bo_rep"] = din("bo_rep", (128, D))
    env["cqw"] = din("cqw", (3, D))
    env["ckw"] = din("ckw", (3, D))
    env["dskip_rep"] = din("dskip_rep", (128, D))
    env["ssmT"] = din("ssmT", (g.KLEN, D))
    env["fwd_cos"] = din("fwd_cos", (g.KC, 128, g.PCH * 128))
    env["fwd_sin"] = din("fwd_sin", (g.KC, 128, g.PCH * 128))
    env["inv_cos"] = din("inv_cos", (g.BINSP, g.NOWN))
    env["inv_sin"] = din("inv_sin", (g.BINSP, g.NOWN))
    env["out"] = nc.dram_tensor("out", [g.NOWN, D], F32, kind="ExternalOutput").ap()

    def scr(name, shape, dt=BF16):
        return nc.dram_tensor(name, list(shape), dt, kind="Internal").ap()

    env["xn_sp"] = scr("xn_sp", (g.SLABP, D))
    env["kv_sp"] = scr("kv_sp", (D, F))
    env["tblf_sp"] = scr("tblf_sp", (2, g.KC, 128, g.PCH * 128))
    env["tbli_sp"] = scr("tbli_sp", (2, g.BINSP, g.NOWN))
    env["khat_sp"] = scr("khat_sp", (2, g.KC, g.GR, 128, g.GSZ))
    env["attn_sp"] = scr("attn_sp", (g.NOWN, D))
    env["w1_bf"] = scr("w1_bf", (g.DFF, D))
    env["w2_bf"] = scr("w2_bf", (g.DFF, D))

    with ExitStack() as S:
        bld = Builder(nc, S)
        psum = S.enter_context(nc.psum_tensor("ps", [128, 4096], F32))
        env["bank"] = lambda i: psum[:, 512 * i:512 * (i + 1)]
        env["drep"] = S.enter_context(nc.sbuf_tensor("drep", [128, D], BF16))

        _phase1(nc, bld, g, env)
        _phase2a(nc, bld, g, env)
        with ExitStack() as S2:
            env["qtm"] = S2.enter_context(
                nc.sbuf_tensor("qtm", [128, g.TCHO * D], BF16))
            _phase2b(nc, bld, g, env)
            with ExitStack() as S3:
                env["y2tm"] = S3.enter_context(
                    nc.sbuf_tensor("y2tm", [128, g.TCHO * D], BF16))
                _phase3(nc, bld, g, env)
                _phase4(nc, bld, g, env)
        _phase5(nc, bld, g, env)

    return nc


def _phase1(nc, bld, g, env):
    rec = Rec()
    bld.begin_phase()
    D, ECH, SLABP = g.D, g.ECH, g.SLABP
    F, KC, PCH, SC, GSZ, GR = g.F, g.KC, g.PCH, g.SC, g.GSZ, g.GR
    bank = env["bank"]
    NTile = SLABP // 128
    BCH = min(512, D)
    NCH = D // BCH
    gX = [bld.dsem(f"p1gX{i}", "gpsimd") for i in range(2)]
    xS = [bld.dsem(f"p1xS{i}", "sync") for i in range(2)]
    gB = bld.dsem("p0gB", "gpsimd")
    gTI = [bld.dsem(f"p0gTI{i}", "gpsimd") for i in range(2)]
    xTI = [bld.dsem(f"p0xTI{i}", "sync") for i in range(2)]
    gTF = [bld.dsem(f"p0gTF{i}", "gpsimd") for i in range(2)]
    xTF = [bld.dsem(f"p0xTF{i}", "sync") for i in range(2)]
    xKB = [bld.dsem(f"p0xKB{i}", "sync") for i in range(2)]
    xZ = bld.dsem("p0xZ", "sync")

    with ExitStack() as P:
        sb = lambda name, shape, dt, **kw: P.enter_context(
            nc.sbuf_tensor(name, list(shape), dt, **kw))
        xb = [sb(f"p1_x{i}", (128, D), F32) for i in range(2)]
        xn = [sb(f"p1_xn{i}", (128, D), BF16) for i in range(2)]
        st6 = sb("p1_st", (128, 6 * NCH), F32)
        mv = sb("p1_mv", (128, 2), F32)
        ivs = sb("p1_ivs", (128, 2), F32)
        ssm_sb = sb("p0_ssm", (128, SC * D), BF16)
        tf = [sb(f"p0_tf{i}", (128, PCH * 128), BF16) for i in range(2)]
        tib = [sb(f"p0_ti{i}", (128, g.NOWN), BF16) for i in range(2)]
        kb = [sb(f"p0_kb{i}", (128, GSZ), BF16) for i in range(2)]
        zt = sb("p0_zero", (128, D), BF16)

        bld.dma(rec, "gpsimd", gB,
                ssm_sb.rearrange("p (sc d) -> p sc d", sc=SC),
                env["ssmT"].rearrange("(sc sp) d -> sp sc d", sp=128))
        bld.wait(rec, "tensor", gB, bld.c[gB])
        zv = bld.tok(rec, "vector", "V", lambda E: E.memset(zt[:, :], 0.0))
        bld.wait(rec, "sync", "V", zv)
        ZC = zt.shape[1]
        for ec in range(D // 128):
            for c0 in range(0, g.p_nz, ZC):
                c1 = min(c0 + ZC, g.p_nz)
                bld.dma(rec, "sync", xZ,
                        env["kv_sp"][ec * 128:(ec + 1) * 128, c0:c1],
                        zt[:, :c1 - c0], slow=True)

        kjobs = [(kc, ri) for kc in range(KC) for ri in range(2)]
        kstate = {"job": 0, "mm_last": {}, "ev_prev": {}}

        def emit_khat(n):
            for _ in range(n):
                if not kjobs:
                    return
                kc, ri = kjobs.pop(0)
                job = kstate["job"]
                tsrc = env["fwd_cos"] if ri == 0 else env["fwd_sin"]
                par = job % 2
                if job >= 2:
                    bld.wait(rec, "gpsimd", xTF[par], bld.c[xTF[par]])
                    bld.wait(rec, "gpsimd", "T", kstate["mm_last"][job - 2])
                v = bld.dma(rec, "gpsimd", gTF[par], tf[par][:, :], tsrc[kc])
                bld.wait(rec, "sync", gTF[par], v)
                bld.dma(rec, "sync", xTF[par],
                        env["tblf_sp"][ri, kc], tf[par][:, :])
                bld.wait(rec, "tensor", gTF[par], v)
                for gg in range(GR):
                    bnk = 6 + (job * GR + gg) % 2
                    if job * GR + gg >= 2:
                        bld.wait(rec, "tensor", "A",
                                 kstate["ev_prev"][(job * GR + gg) - 2])

                    def p0mm(E, b, par, sc, gg):
                        return E.matmul(
                            bank(b)[:, :GSZ],
                            tf[par][:, sc * 128:(sc + 1) * 128],
                            ssm_sb[:, sc * D + gg * GSZ:sc * D + gg * GSZ + GSZ],
                            start=(sc == 0), stop=(sc == SC - 1))
                    for sc in range(SC - 1):
                        bld.run(rec, "tensor",
                                lambda E, b=bnk, par=par, sc=sc, gg=gg, f=p0mm:
                                f(E, b, par, sc, gg))
                    mt = bld.tok(rec, "tensor", "T",
                                 lambda E, b=bnk, par=par, gg=gg, f=p0mm:
                                 f(E, b, par, SC - 1, gg))
                    bld.wait(rec, "scalar", "T", mt)
                    kpar = (job * GR + gg) % 2
                    bld.wait(rec, "scalar", xKB[kpar], bld.c[xKB[kpar]])
                    ev = bld.tok(rec, "scalar", "A",
                                 lambda E, b=bnk, kpar=kpar:
                                 E.copy(kb[kpar][:, :], bank(b)[:, :GSZ]))
                    kstate["ev_prev"][job * GR + gg] = ev
                    bld.wait(rec, "sync", "A", ev)
                    bld.dma(rec, "sync", xKB[kpar],
                            env["khat_sp"][ri, kc, gg], kb[kpar][:, :])
                kstate["mm_last"][job] = bld.c["T"]
                kstate["job"] = job + 1

        ijobs = list(range(2 * KC))

        def emit_itab(n):
            for _ in range(n):
                if not ijobs:
                    return
                idx = ijobs.pop(0)
                ri, kc = idx // KC, idx % KC
                isrc = env["inv_cos"] if ri == 0 else env["inv_sin"]
                par = idx % 2
                if idx >= 2:
                    bld.wait(rec, "gpsimd", xTI[par], bld.c[xTI[par]])
                v = bld.dma(rec, "gpsimd", gTI[par], tib[par][:, :],
                            isrc[kc * 128:(kc + 1) * 128, :])
                bld.wait(rec, "sync", gTI[par], v)
                bld.dma(rec, "sync", xTI[par],
                        env["tbli_sp"][ri, kc * 128:(kc + 1) * 128, :],
                        tib[par][:, :])

        KPT = (2 * KC + NTile - 1) // NTile

        nrm_tok = {}
        for rt in range(NTile):
            par = rt % 2
            if rt >= 2:
                bld.wait(rec, "gpsimd", "V", nrm_tok[rt - 2])
            vld = bld.dma(rec, "gpsimd", gX[par], xb[par][:, :],
                          env["x_slab"][rt * 128:(rt + 1) * 128, :])
            bld.wait(rec, "vector", gX[par], vld)
            if rt >= 2:
                bld.wait(rec, "vector", xS[par], bld.c[xS[par]])
            for i in range(NCH):
                bld.run(rec, "vector", lambda E, par=par, i=i:
                        E.bn_stats(st6[:, 6 * i:6 * (i + 1)],
                                   xb[par][:, BCH * i:BCH * (i + 1)]))
            bld.run(rec, "vector", lambda E: E.bn_aggr(mv[:, 0:2], st6[:, :]))
            bld.run(rec, "vector", lambda E:
                    E.tensor_scalar_add(ivs[:, 0:1], mv[:, 1:2], 1e-5))
            va = bld.tok(rec, "vector", "V", lambda E:
                         E.reciprocal(ivs[:, 1:2], ivs[:, 0:1]))
            bld.wait(rec, "scalar", "V", va)
            sa = bld.tok(rec, "scalar", "A", lambda E:
                         E.activation(ivs[:, 0:1], ivs[:, 1:2], AF.Sqrt))
            bld.wait(rec, "vector", "A", sa)
            nrm_tok[rt] = bld.tok(rec, "vector", "V", lambda E, par=par:
                                  E.tensor_scalar(xn[par][:, :], xb[par][:, :],
                                                  mv[:, 0:1], ivs[:, 0:1],
                                                  OP.subtract, OP.mult))
            bld.wait(rec, "sync", "V", nrm_tok[rt])
            if rt >= 2:
                bld.wait(rec, "sync", xS[par], bld.c[xS[par]])
            bld.dma(rec, "sync", xS[par],
                    env["xn_sp"][rt * 128:(rt + 1) * 128, :], xn[par][:, :])
            emit_khat(KPT)
            emit_itab(KPT)
        emit_khat(len(kjobs))
        emit_itab(len(ijobs))
        bld.drain(rec)
        rec.replay(nc)


def _phase2a(nc, bld, g, env):
    rec = Rec()
    bld.begin_phase()
    D, ECH, TT, SLABP = g.D, g.ECH, g.TT, g.SLABP
    bank = env["bank"]
    TW = TT + 2
    gB = bld.dsem("p2gB", "gpsimd")
    gXQ = [bld.dsem(f"p2gXQ{i}", "sync") for i in range(2)]
    xKVT = [[bld.dsem(f"p2xKVT{q}{p}", "gpsimd") for p in range(2)]
            for q in range(2)]

    with ExitStack() as P:
        sb = lambda name, shape, dt=BF16, **kw: P.enter_context(
            nc.sbuf_tensor(name, list(shape), dt, **kw))
        wk = sb("p2_wk", (128, ECH * D))
        wv = sb("p2_wv", (128, ECH * D))
        XSP = 2 * TT
        xq = [P.enter_context(nc.sbuf_tensor(
            f"p2_xq{i}", [128, ECH * XSP], BF16, is_transpose_dest=True))
            for i in range(2)]
        kcm = [sb(f"p2_kcm{i}", (128, ECH * TW)) for i in range(2)]
        vcm = [sb(f"p2_vcm{i}", (128, ECH * TT)) for i in range(2)]
        kvo = sb("p2_kvo", (128, ECH * TT))
        cvt = sb("p2_cvt", (128, TT))
        cvt2 = sb("p2_cvt2", (128, TT))
        bks = sb("p2_bk", (128, ECH), F32)
        bvs = sb("p2_bv", (128, ECH), F32)
        cks = sb("p2_ck", (128, 3 * ECH), F32)
        ckbs = sb("p2_ckb", (128, ECH), F32)

        bld.dma(rec, "gpsimd", gB, wk.rearrange("p (hc e) -> p hc e", hc=ECH),
                env["wkT"].rearrange("(hc hp) e -> hp hc e", hp=128))
        bld.dma(rec, "gpsimd", gB, wv.rearrange("p (hc e) -> p hc e", hc=ECH),
                env["wvT"].rearrange("(hc hp) e -> hp hc e", hp=128))
        bld.dma(rec, "gpsimd", gB, bks[:, :],
                env["bk"].rearrange("(ec ep) -> ep ec", ep=128), slow=True)
        bld.dma(rec, "gpsimd", gB, bvs[:, :],
                env["bv"].rearrange("(ec ep) -> ep ec", ep=128), slow=True)
        for j in range(3):
            bld.dma(rec, "gpsimd", gB, cks[:, j * ECH:(j + 1) * ECH],
                    env["ckw"][j].rearrange("(ec ep) -> ep ec", ep=128), slow=True)
        bld.dma(rec, "gpsimd", gB, ckbs[:, :],
                env["ckb"].rearrange("(ec ep) -> ep ec", ep=128), slow=True)
        gB_total = bld.c[gB]

        bld.run(rec, "vector", lambda E:
                E.memset(kcm[0].rearrange("p (ec t) -> p ec t", ec=ECH)[:, :, 0:2],
                         0.0))
        z0 = bld.tok(rec, "vector", "V", lambda E:
                     E.memset(kcm[1].rearrange("p (ec t) -> p ec t",
                                               ec=ECH)[:, :, 0:2], 0.0))

        xq_val = {}
        NTile = SLABP // TT
        ev_tok, kv_tok, mm_tok, xb_val, sp_val = {}, {}, {}, {}, {}
        job = 0
        for ti in range(NTile):
            par = ti % 2
            if ti >= 4 and ti % 2 == 0:
                bld.wait(rec, "sync", "T", mm_tok[ti - 3])
            qpar = (ti // 2) % 2
            if ti % 2 == 0:
                xw = min(XSP, SLABP - ti * TT)
                for hc in range(ECH):
                    vld = bld.dma(rec, "sync", gXQ[qpar],
                                  xq[qpar][:, hc * XSP:hc * XSP + xw],
                                  env["xn_sp"][ti * TT:ti * TT + xw,
                                               hc * 128:(hc + 1) * 128],
                                  transpose=True, slow=True)
                xq_val[ti // 2] = vld
            bld.wait(rec, "tensor", gXQ[qpar], xq_val[ti // 2])

            if ti == 0:
                bld.wait(rec, "tensor", gB, gB_total)
                bld.wait(rec, "scalar", "V", z0)
            if ti >= 2:
                bld.wait(rec, "scalar", "V", kv_tok[ti - 2])
            for wsb, dst, bias, off in ((wk, kcm[par], bks, 2),
                                        (wv, vcm[par], bvs, 0)):
                for ec in range(ECH):
                    bnk = job % 2
                    if job >= 2:
                        bld.wait(rec, "tensor", "A", ev_tok[job - 2])

                    qoff = (ti % 2) * TT

                    def p2mm(E, b, w, hc, qpar, ec, qoff=qoff):
                        return E.matmul(
                            bank(b)[:, :TT],
                            w[:, hc * D + ec * 128:hc * D + ec * 128 + 128],
                            xq[qpar][:, hc * XSP + qoff:hc * XSP + qoff + TT],
                            start=(hc == 0), stop=(hc == ECH - 1))
                    for hc in range(ECH - 1):
                        bld.run(rec, "tensor",
                                lambda E, b=bnk, w=wsb, hc=hc, qpar=qpar, ec=ec,
                                f=p2mm: f(E, b, w, hc, qpar, ec))
                    mt = bld.tok(rec, "tensor", "T",
                                 lambda E, b=bnk, w=wsb, qpar=qpar, ec=ec,
                                 f=p2mm: f(E, b, w, ECH - 1, qpar, ec))
                    bld.wait(rec, "scalar", "T", mt)
                    TWW = TW if dst is kcm[par] else TT
                    ev_tok[job] = bld.tok(
                        rec, "scalar", "A",
                        lambda E, b=bnk, dst=dst, ec=ec, off=off, TWW=TWW, bias=bias:
                        E.activation(dst[:, ec * TWW + off:ec * TWW + off + TT],
                                     bank(b)[:, :TT], AF.Identity,
                                     bias=bias[:, ec:ec + 1], scale=1.0))
                    job += 1
            mm_tok[ti] = bld.c["T"]
            bld.wait(rec, "vector", "A", ev_tok[job - 1])
            if ti >= 1:
                bld.wait(rec, "vector", xKVT[0][(ti - 1) % 2], xb_val[ti - 1][0])
                bld.wait(rec, "vector", xKVT[1][(ti - 1) % 2], xb_val[ti - 1][1])
            for ec in range(ECH):
                base = ec * TW
                bld.run(rec, "vector", lambda E, base=base, ec=ec, par=par:
                        E.tensor_scalar(cvt[:, :TT], kcm[par][:, base:base + TT],
                                        cks[:, ec:ec + 1],
                                        ckbs[:, ec:ec + 1], OP.mult, OP.add))
                bld.run(rec, "vector", lambda E, base=base, ec=ec, par=par:
                        E.scalar_tensor_tensor(cvt2[:, :TT],
                                               kcm[par][:, base + 1:base + 1 + TT],
                                               cks[:, ECH + ec:ECH + ec + 1],
                                               cvt[:, :TT], OP.mult, OP.add))
                bld.run(rec, "vector", lambda E, base=base, ec=ec, par=par:
                        E.scalar_tensor_tensor(cvt[:, :TT],
                                               kcm[par][:, base + 2:base + 2 + TT],
                                               cks[:, 2 * ECH + ec:2 * ECH + ec + 1],
                                               cvt2[:, :TT], OP.mult, OP.add))
                bld.run(rec, "vector", lambda E, ec=ec, par=par:
                        E.tensor_mul(kvo[:, ec * TT:(ec + 1) * TT],
                                     cvt[:, :TT],
                                     vcm[par][:, ec * TT:(ec + 1) * TT]))
            kv_tok[ti] = bld.tok(rec, "vector", "V", lambda E, par=par:
                                 E.tensor_copy(
                                     kcm[1 - par].rearrange(
                                         "p (ec t) -> p ec t", ec=ECH)[:, :, 0:2],
                                     kcm[par].rearrange(
                                         "p (ec t) -> p ec t",
                                         ec=ECH)[:, :, TT:TT + 2]))
            bld.wait(rec, "gpsimd", "V", kv_tok[ti])
            poff = g.F - g.NOWN - g.KLEN - 1
            r0 = ti * TT
            c0, c1 = max(r0, 2), min(r0 + TT, g.SLAB)
            if c1 > c0:
                for ec in range(ECH):
                    qi = ec % 2
                    bld.dma(rec, "gpsimd", xKVT[qi][par],
                            env["kv_sp"][ec * 128:(ec + 1) * 128,
                                         c0 + poff:c1 + poff],
                            kvo[:, ec * TT + (c0 - r0):ec * TT + (c1 - r0)],
                            slow=(c1 - c0 < 16))
            xb_val[ti] = (bld.c[xKVT[0][par]], bld.c[xKVT[1][par]])
        bld.drain(rec)
        rec.replay(nc)


def _phase2b(nc, bld, g, env):
    rec = Rec()
    bld.begin_phase()
    D, ECH, QW, NOWN = g.D, g.ECH, g.QW, g.NOWN
    bank = env["bank"]
    gB = bld.dsem("p2bgB", "gpsimd")
    xQ = bld.dsem("p2bxQ", "sync")
    xQ2 = bld.dsem("p2bxQ2", "sync")

    with ExitStack() as P:
        sb = lambda name, shape, dt=BF16, **kw: P.enter_context(
            nc.sbuf_tensor(name, list(shape), dt, **kw))
        wq = sb("p2b_wq", (128, ECH * D))
        QWT = ((QW + 15) // 16) * 16
        xq = P.enter_context(nc.sbuf_tensor(
            "p2b_xq", [128, ECH * QWT], BF16, is_transpose_dest=True))
        qpre = sb("p2b_qpre", (128, ECH * QW))
        qcv = sb("p2b_qcv", (128, ECH * NOWN))
        cvt = sb("p2b_cvt", (128, 512), F32)
        cvt2 = sb("p2b_cvt2", (128, 512), F32)
        bqs = sb("p2b_bq", (128, ECH), F32)
        cqs = sb("p2b_cq", (128, 3 * ECH), F32)
        cqbs = sb("p2b_cqb", (128, ECH), F32)

        bld.dma(rec, "gpsimd", gB, wq.rearrange("p (hc e) -> p hc e", hc=ECH),
                env["wqT"].rearrange("(hc hp) e -> hp hc e", hp=128))
        bld.dma(rec, "gpsimd", gB, bqs[:, :],
                env["bq"].rearrange("(ec ep) -> ep ec", ep=128), slow=True)
        for j in range(3):
            bld.dma(rec, "gpsimd", gB, cqs[:, j * ECH:(j + 1) * ECH],
                    env["cqw"][j].rearrange("(ec ep) -> ep ec", ep=128), slow=True)
        bld.dma(rec, "gpsimd", gB, cqbs[:, :],
                env["cqb"].rearrange("(ec ep) -> ep ec", ep=128), slow=True)
        c0 = g.R0 - 2
        QWT = ((QW + 15) // 16) * 16
        for hc in range(ECH):
            bld.dma(rec, "sync", xQ2, xq[:, hc * QWT:hc * QWT + QWT],
                    env["xn_sp"][c0:c0 + QWT, hc * 128:(hc + 1) * 128],
                    transpose=True, slow=True)
        bld.wait(rec, "tensor", gB, bld.c[gB])
        bld.wait(rec, "tensor", xQ2, bld.c[xQ2])

        ev_tok = {}
        job = 0
        CQ = min(512, NOWN)
        QWT = ((QW + 15) // 16) * 16
        for ec in range(ECH):
            co = 0
            for w in g.QCHUNKS:
                bnk = job % 2
                if job >= 2:
                    bld.wait(rec, "tensor", "A", ev_tok[job - 2])

                def p2bmm(E, b, hc, ec, co, w):
                    return E.matmul(
                        bank(b)[:, :w],
                        wq[:, hc * D + ec * 128:hc * D + ec * 128 + 128],
                        xq[:, hc * QWT + co:hc * QWT + co + w],
                        start=(hc == 0), stop=(hc == ECH - 1))
                for hc in range(ECH - 1):
                    bld.run(rec, "tensor",
                            lambda E, b=bnk, hc=hc, ec=ec, co=co, w=w,
                            f=p2bmm: f(E, b, hc, ec, co, w))
                mt = bld.tok(rec, "tensor", "T",
                             lambda E, b=bnk, ec=ec, co=co, w=w, f=p2bmm:
                             f(E, b, ECH - 1, ec, co, w))
                bld.wait(rec, "scalar", "T", mt)
                ev_tok[job] = bld.tok(
                    rec, "scalar", "A",
                    lambda E, b=bnk, ec=ec, co=co, w=w:
                    E.activation(qpre[:, ec * QW + co:ec * QW + co + w],
                                 bank(b)[:, :w], AF.Identity,
                                 bias=bqs[:, ec:ec + 1], scale=1.0))
                co += w
                job += 1
            bld.wait(rec, "vector", "A", ev_tok[job - 1])
            base = ec * QW
            qv = None
            for jt in range(NOWN // CQ):
                o = jt * CQ
                bld.run(rec, "vector", lambda E, base=base, ec=ec, o=o:
                        E.tensor_scalar(cvt[:, :CQ],
                                        qpre[:, base + o:base + o + CQ],
                                        cqs[:, ec:ec + 1],
                                        cqbs[:, ec:ec + 1], OP.mult, OP.add))
                bld.run(rec, "vector", lambda E, base=base, ec=ec, o=o:
                        E.scalar_tensor_tensor(cvt2[:, :CQ],
                                               qpre[:, base + o + 1:base + o + 1 + CQ],
                                               cqs[:, ECH + ec:ECH + ec + 1],
                                               cvt[:, :CQ], OP.mult, OP.add))
                qv = bld.tok(rec, "vector", "V",
                             lambda E, base=base, ec=ec, o=o:
                             E.scalar_tensor_tensor(
                                 qcv[:, ec * NOWN + o:ec * NOWN + o + CQ],
                                 qpre[:, base + o + 2:base + o + 2 + CQ],
                                 cqs[:, 2 * ECH + ec:2 * ECH + ec + 1],
                                 cvt2[:, :CQ], OP.mult, OP.add))
            bld.wait(rec, "sync", "V", qv)
            for tc in range(g.TCHO):
                bld.dma(rec, "sync", xQ,
                        env["qtm"][:, tc * D + ec * 128:tc * D + ec * 128 + 128],
                        qcv[:, ec * NOWN + tc * 128:ec * NOWN + tc * 128 + 128],
                        transpose=True)
        bld.drain(rec)
        rec.replay(nc)


def _phase3(nc, bld, g, env):
    rec = Rec()
    bld.begin_phase()
    D, F, KC, PCH, GSZ, GR, NOWN, TCHO = g.D, g.F, g.KC, g.PCH, g.GSZ, g.GR, \
        g.NOWN, g.TCHO
    bank = env["bank"]
    PCO = PCH - TCHO
    gKV = [bld.dsem(f"p3gKV{i}", "sync") for i in range(2)]
    TFD = 2
    gTF = [bld.dsem(f"p3gTF{i}", "gpsimd") for i in range(TFD)]
    gKH = [bld.dsem(f"p3gKH{i}", "gpsimd") for i in range(2)]
    TID = 4
    gTI = [bld.dsem(f"p3gTI{i}", "gpsimd") for i in range(TID)]
    gWC = [bld.dsem(f"p3gWC{i}", "gpsimd") for i in range(2)]
    xWC = [bld.dsem(f"p3xWC{i}", "sync") for i in range(2)]
    gDR = bld.dsem("p3gDR", "gpsimd")

    with ExitStack() as P:
        sb = lambda name, shape, dt=BF16: P.enter_context(
            nc.sbuf_tensor(name, list(shape), dt))
        kvg = [P.enter_context(nc.sbuf_tensor(
            f"p3_kvg{i}", [128, PCH * GSZ], BF16, is_transpose_dest=True))
            for i in range(2)]
        tf = [sb(f"p3_tf{i}", (128, PCH * 128)) for i in range(TFD)]
        X = [sb("p3_xr", (128, KC * GSZ)), sb("p3_xi", (128, KC * GSZ))]
        Z = X
        Kh = [[sb(f"p3_kr{i}", (128, GSZ)), sb(f"p3_ki{i}", (128, GSZ))]
              for i in range(2)]
        tmp = sb("p3_tmp", (128, GSZ))
        tmp2 = sb("p3_tmp2", (128, GSZ))
        tmp3 = sb("p3_tmp3", (128, GSZ))
        tmp4 = sb("p3_tmp4", (128, GSZ))
        ti = [sb(f"p3_ti{i}", (128, NOWN)) for i in range(TID)]
        t1 = sb("p3_t1", (128, GSZ))
        t2 = sb("p3_t2", (128, GSZ), F32)
        wcb = [sb(f"p3_wc{i}", (128, 1024)) for i in range(2)]

        bld.dma(rec, "gpsimd", gDR, env["drep"][:, :], env["dskip_rep"][:, :])
        bld.wait(rec, "vector", gDR, 16)
        wjobs = []
        for name, dst, rows, cols in (("w1_lay", "w1_bf", g.DFF, D),):
            CW = min(1024, cols)
            for r in range(rows // 128):
                for cc in range(cols // CW):
                    wjobs.append((name, dst, r, cc, CW))
        wj_seen = [0]
        wc_val = {0: 0, 1: 0}

        def emit_wjobs(n):
            for _ in range(n):
                if not wjobs:
                    return
                name, dst, r, cc, CW = wjobs.pop(0)
                wpar = len(wjobs) % 2
                if wj_seen[0] >= 2:
                    bld.wait(rec, "gpsimd", xWC[wpar], wc_val[wpar])
                v = bld.dma(rec, "gpsimd", gWC[wpar], wcb[wpar][:, :CW],
                            env[name][r * 128:(r + 1) * 128,
                                      cc * CW:(cc + 1) * CW])
                bld.wait(rec, "sync", gWC[wpar], v)
                bld.dma(rec, "sync", xWC[wpar],
                        env[dst][r * 128:(r + 1) * 128, cc * CW:(cc + 1) * CW],
                        wcb[wpar][:, :CW])
                wc_val[wpar] = bld.c[xWC[wpar]]
                wj_seen[0] += 1

        ev_tok, ld_tok, inv_mm = {}, {}, {}
        kv_need = {}
        y2v_hist = {}
        y2v = None
        for gg in range(GR):
            kp3 = gg % 2
            if gg == 0:
                for pc in range(PCH):
                    bld.dma(rec, "sync", gKV[0],
                            kvg[0][:, pc * GSZ:(pc + 1) * GSZ],
                            env["kv_sp"][0:GSZ, pc * 128:(pc + 1) * 128],
                            transpose=True, slow=True)
                kv_need[0] = bld.c[gKV[0]]
            bld.wait(rec, "tensor", gKV[kp3], kv_need[gg])
            job = 0
            for kc in range(KC):
                for ri in range(2):
                    gjob = gg * 2 * KC + job
                    par = gjob % TFD
                    eng = "gpsimd"
                    if gjob - TFD >= 0:
                        bld.wait(rec, eng, "T", ld_tok[gjob - TFD])
                    v = bld.dma(rec, eng, gTF[par],
                                tf[par][:, :], env["tblf_sp"][ri, kc])
                    emit_wjobs(1)
                    bld.wait(rec, "tensor", gTF[par], v)
                    bnk = job % 2
                    if job >= 2:
                        bld.wait(rec, "tensor", "V", ev_tok[(gg, job - 2)])

                    def p3mm(E, b, par, pc, kp3=kp3):
                        return E.matmul(bank(b)[:, :GSZ],
                                        tf[par][:, pc * 128:(pc + 1) * 128],
                                        kvg[kp3][:, pc * GSZ:(pc + 1) * GSZ],
                                        start=(pc == g.PC0), stop=(pc == PCH - 1))
                    for pc in range(g.PC0, PCH - 1):
                        bld.run(rec, "tensor",
                                lambda E, b=bnk, par=par, pc=pc, f=p3mm:
                                f(E, b, par, pc))
                    mt = bld.tok(rec, "tensor", "T",
                                 lambda E, b=bnk, par=par, f=p3mm:
                                 f(E, b, par, PCH - 1))
                    ld_tok[gjob] = mt
                    bld.wait(rec, "vector", "T", mt)
                    ev_tok[(gg, job)] = bld.tok(
                        rec, "vector", "V",
                        lambda E, b=bnk, ri=ri, kc=kc:
                        E.tensor_copy(X[ri][:, kc * GSZ:(kc + 1) * GSZ],
                                      bank(b)[:, :GSZ]))
                    job += 1
            if gg + 1 < GR:
                if gg >= 1:
                    bld.wait(rec, "sync", "V", y2v_hist[gg - 1])
                for pc in range(PCH):
                    bld.dma(rec, "sync", gKV[(gg + 1) % 2],
                            kvg[(gg + 1) % 2][:, pc * GSZ:(pc + 1) * GSZ],
                            env["kv_sp"][(gg + 1) * GSZ:(gg + 2) * GSZ,
                                         pc * 128:(pc + 1) * 128],
                            transpose=True, slow=True)
                kv_need[gg + 1] = bld.c[gKV[(gg + 1) % 2]]
            pw_tok = None
            pw_done = {}
            for kc in range(KC):
                kp = kc % 2
                if kc >= 2:
                    bld.wait(rec, "gpsimd", "V", pw_done[kc - 2])
                elif gg >= 1:
                    last_same = KC - 1 - ((KC - 1 - kc) % 2)
                    bld.wait(rec, "gpsimd", "V", pw_prev[last_same])
                for ri in range(2):
                    bld.dma(rec, "gpsimd", gKH[kp],
                            Kh[kp][ri][:, :], env["khat_sp"][ri, kc, gg])
                bld.wait(rec, "vector", gKH[kp], bld.c[gKH[kp]])
                s = slice(kc * GSZ, (kc + 1) * GSZ)
                bld.run(rec, "vector", lambda E, s=s, kp=kp:
                        E.tensor_mul(tmp[:, :], X[0][:, s], Kh[kp][0][:, :]))
                bld.run(rec, "vector", lambda E, s=s, kp=kp:
                        E.tensor_mul(tmp2[:, :], X[1][:, s], Kh[kp][1][:, :]))
                bld.run(rec, "vector", lambda E, s=s, kp=kp:
                        E.tensor_mul(tmp3[:, :], X[0][:, s], Kh[kp][1][:, :]))
                bld.run(rec, "vector", lambda E, s=s, kp=kp:
                        E.tensor_mul(tmp4[:, :], X[1][:, s], Kh[kp][0][:, :]))
                bld.run(rec, "vector", lambda E, s=s:
                        E.tensor_sub(Z[0][:, s], tmp[:, :], tmp2[:, :]))
                pw_tok = bld.tok(rec, "vector", "V", lambda E, s=s:
                                 E.tensor_add(Z[1][:, s], tmp3[:, :], tmp4[:, :]))
                pw_done[kc] = pw_tok
            pw_prev = pw_done
            bld.wait(rec, "tensor", "V", pw_tok)
            first = True
            for kc in range(KC):
                for ri in range(2):
                    idx = kc * 2 + ri
                    gidx = gg * 2 * KC + idx
                    par = gidx % TID
                    eng = "gpsimd"
                    if gidx - TID >= 0:
                        bld.wait(rec, eng, "T", inv_mm[gidx - TID])
                    v = bld.dma(rec, eng, gTI[par], ti[par][:, :],
                                env["tbli_sp"][ri, kc * 128:(kc + 1) * 128, :])
                    bld.wait(rec, "tensor", gTI[par], v)
                    last = (kc == KC - 1) and (ri == 1)

                    def pinvmm(E, tc, par=par, ri=ri, kc=kc, first=first,
                               last=last):
                        return E.matmul(bank(tc)[:, :GSZ],
                                        ti[par][:, tc * 128:(tc + 1) * 128],
                                        Z[ri][:, kc * GSZ:(kc + 1) * GSZ],
                                        start=first, stop=last)
                    for tc in range(TCHO - 1):
                        bld.run(rec, "tensor", lambda E, tc=tc, f=pinvmm: f(E, tc))
                    inv_mm[gidx] = bld.tok(
                        rec, "tensor", "T", lambda E, f=pinvmm: f(E, TCHO - 1))
                    first = False
            mtv = bld.c["T"]
            bld.wait(rec, "vector", "T", mtv)
            for tc in range(TCHO):
                kvs = slice((PCO + tc) * GSZ, (PCO + tc + 1) * GSZ)
                bld.run(rec, "vector", lambda E, kvs=kvs, gg=gg, kp3=kp3:
                        E.tensor_mul(t1[:, :], kvg[kp3][:, kvs],
                                     env["drep"][:, gg * GSZ:(gg + 1) * GSZ]))
                bld.run(rec, "vector", lambda E, tc=tc:
                        E.tensor_add(t2[:, :], bank(tc)[:, :GSZ], t1[:, :]))
                qs = slice(tc * D + gg * GSZ, tc * D + (gg + 1) * GSZ)

                def y2mul(E, qs=qs):
                    return E.tensor_mul(env["y2tm"][:, qs], t2[:, :],
                                        env["qtm"][:, qs])
                if tc == TCHO - 1:
                    y2v = bld.tok(rec, "vector", "V", y2mul)
                else:
                    bld.run(rec, "vector", y2mul)
            y2v_hist[gg] = y2v
            y2v_hist[gg] = y2v
            bld.wait(rec, "tensor", "V", y2v)
        emit_wjobs(len(wjobs))
        bld.drain(rec)
        rec.replay(nc)


def _phase4(nc, bld, g, env):
    rec = Rec()
    bld.begin_phase()
    D, ECH, NOWN, TCHO = g.D, g.ECH, g.NOWN, g.TCHO
    bank = env["bank"]
    EOSZ = min(512, D)
    EON = D // EOSZ
    gWO = bld.dsem("p4gWO", "gpsimd")
    xY = bld.dsem("p4xY", "sync")
    xAB = [bld.dsem(f"p4xAB{i}", "sync") for i in range(2)]

    with ExitStack() as P:
        sb = lambda name, shape, dt=BF16, **kw: P.enter_context(
            nc.sbuf_tensor(name, list(shape), dt, **kw))
        y2fm = sb("p4_y2fm", (128, ECH * NOWN), is_transpose_dest=True)
        wo = sb("p4_wo", (128, ECH * D))
        ab = [sb(f"p4_ab{i}", (128, EOSZ)) for i in range(2)]

        bld.dma(rec, "gpsimd", gWO, wo.rearrange("p (hc e) -> p hc e", hc=ECH),
                env["woT"].rearrange("(hc hp) e -> hp hc e", hp=128))
        for tc in range(TCHO):
            for ec in range(ECH):
                bld.dma(rec, "sync", xY,
                        y2fm[:, ec * NOWN + tc * 128:ec * NOWN + tc * 128 + 128],
                        env["y2tm"][:, tc * D + ec * 128:tc * D + ec * 128 + 128],
                        transpose=True)
        bld.wait(rec, "tensor", xY, bld.c[xY])
        bld.wait(rec, "tensor", gWO, bld.c[gWO])
        job = 0
        ev_tok = {}
        for tc in range(TCHO):
            for eo in range(EON):
                bnk = job % 2
                if job >= 2:
                    bld.wait(rec, "tensor", "A", ev_tok[job - 2])

                def p4mm(E, b, hc, tc, eo):
                    return E.matmul(
                        bank(b)[:, :EOSZ],
                        y2fm[:, hc * NOWN + tc * 128:hc * NOWN + tc * 128 + 128],
                        wo[:, hc * D + eo * EOSZ:hc * D + (eo + 1) * EOSZ],
                        start=(hc == 0), stop=(hc == ECH - 1))
                for hc in range(ECH - 1):
                    bld.run(rec, "tensor", lambda E, b=bnk, hc=hc, tc=tc, eo=eo:
                            p4mm(E, b, hc, tc, eo))
                mt = bld.tok(rec, "tensor", "T", lambda E, b=bnk, tc=tc, eo=eo:
                             p4mm(E, b, ECH - 1, tc, eo))
                bld.wait(rec, "scalar", "T", mt)
                kpar = job % 2
                bld.wait(rec, "scalar", xAB[kpar], bld.c[xAB[kpar]])
                ev_tok[job] = bld.tok(rec, "scalar", "A",
                                      lambda E, b=bnk, kpar=kpar:
                                      E.copy(ab[kpar][:, :], bank(b)[:, :EOSZ]))
                bld.wait(rec, "sync", "A", ev_tok[job])
                bld.dma(rec, "sync", xAB[kpar],
                        env["attn_sp"][tc * 128:(tc + 1) * 128,
                                       eo * EOSZ:(eo + 1) * EOSZ],
                        ab[kpar][:, :])
                job += 1
        bld.drain(rec)
        rec.replay(nc)


def _phase5(nc, bld, g, env):
    rec = Rec()
    bld.begin_phase()
    D, DFF, ECH, FCH, NOWN, TCHO = g.D, g.DFF, g.ECH, g.FCH, g.NOWN, g.TCHO
    bank = env["bank"]
    TJ = min(512, NOWN)
    NTJ = NOWN // TJ
    EOSZ = min(512, D)
    EON = D // EOSZ
    gB = bld.dsem("p5gB", "gpsimd")
    xXN = bld.dsem("p5xXN", "sync")
    gW1 = [bld.dsem(f"p5gW1{i}", ("gpsimd", "sync")[i % 2]) for i in range(2)]
    gW2 = [bld.dsem(f"p5gW2{i}", "gpsimd") for i in range(2)]
    gXR = [bld.dsem(f"p5gXR{i}", "gpsimd") for i in range(2)]
    xO = [bld.dsem(f"p5xO{i}", "sync") for i in range(2)]

    with ExitStack() as P:
        sb = lambda name, shape, dt=BF16, **kw: P.enter_context(
            nc.sbuf_tensor(name, list(shape), dt, **kw))
        xn = P.enter_context(nc.sbuf_tensor(
            "p5_xn", [128, ECH * NOWN], BF16, is_transpose_dest=True))
        a = sb("p5_a", (128, FCH * NOWN))
        w1c = [sb(f"p5_w1{i}", (128, ECH * 128)) for i in range(2)]
        w2c = [sb(f"p5_w2{i}", (128, EOSZ)) for i in range(2)]
        ub = sb("p5_u", (128, TJ), F32)
        iib = sb("p5_ii", (128, TJ), mybir.dt.int32)
        mb = sb("p5_m", (128, TJ), F32)
        fb = sb("p5_f", (128, TJ), F32)
        cb = sb("p5_c", (128, TJ), F32)
        zb = sb("p5_z", (128, TJ), F32)
        xr = [sb(f"p5_xr{i}", (128, EOSZ), F32) for i in range(2)]
        ar = [sb(f"p5_ar{i}", (128, EOSZ)) for i in range(2)]
        ob = [sb(f"p5_ob{i}", (128, EOSZ), F32) for i in range(2)]
        brp = sb("p5_bo", (128, D), F32)
        t3 = sb("p5_t3", (128, EOSZ), F32)

        bld.dma(rec, "gpsimd", gB, brp[:, :], env["bo_rep"][:, :])
        for hc in range(ECH):
            bld.dma(rec, "sync", xXN, xn[:, hc * NOWN:(hc + 1) * NOWN],
                    env["xn_sp"][g.R0:g.R0 + NOWN, hc * 128:(hc + 1) * 128],
                    transpose=True, slow=True)
        bld.wait(rec, "tensor", gB, bld.c[gB])
        bld.wait(rec, "tensor", xXN, bld.c[xXN])
        job = 0
        ev_tok, w1_tok = {}, {}
        ga = None
        for fc in range(FCH):
            par = fc % 2
            if fc >= 2:
                bld.wait(rec, ("gpsimd", "sync")[par % 2], "T", w1_tok[fc - 2])
            v = bld.dma(rec, ("gpsimd", "sync")[par % 2], gW1[par],
                        w1c[par][:, :],
                        env["w1_bf"][fc * 128:(fc + 1) * 128, :])
            bld.wait(rec, "tensor", gW1[par], v)
            for tj in range(NTJ):
                bnk = job % 2
                if job >= 2:
                    bld.wait(rec, "tensor", "V", ev_tok[job - 2])

                def p5mm(E, b, par, hc, tj):
                    return E.matmul(
                        bank(b)[:, :TJ],
                        w1c[par][:, hc * 128:(hc + 1) * 128],
                        xn[:, hc * NOWN + tj * TJ:hc * NOWN + (tj + 1) * TJ],
                        start=(hc == 0), stop=(hc == ECH - 1))
                for hc in range(ECH - 1):
                    bld.run(rec, "tensor", lambda E, b=bnk, par=par, hc=hc, tj=tj:
                            p5mm(E, b, par, hc, tj))
                mt = bld.tok(rec, "tensor", "T", lambda E, b=bnk, par=par, tj=tj:
                             p5mm(E, b, par, ECH - 1, tj))
                bld.wait(rec, "vector", "T", mt)
                if ga is not None:
                    bld.wait(rec, "vector", "A", ga)
                ev_tok[job] = bld.tok(rec, "vector", "V", lambda E, b=bnk:
                        E.tensor_scalar(ub[:, :], bank(b)[:, :TJ], 10.0, 4096.5,
                                        OP.mult, OP.add))
                bld.run(rec, "vector", lambda E:
                        E.tensor_copy(iib[:, :], ub[:, :]))
                bld.run(rec, "vector", lambda E:
                        E.tensor_copy(fb[:, :], iib[:, :]))
                bld.run(rec, "vector", lambda E:
                        E.tensor_tensor(cb[:, :], fb[:, :], ub[:, :], OP.is_gt))
                bld.run(rec, "vector", lambda E:
                        E.tensor_sub(mb[:, :], fb[:, :], cb[:, :]))
                bld.run(rec, "vector", lambda E:
                        E.tensor_scalar(cb[:, :], mb[:, :], 4096.0, -32768.0,
                                        OP.subtract, OP.max))
                qv = bld.tok(rec, "vector", "V", lambda E:
                             E.tensor_scalar(zb[:, :], cb[:, :], 32767.0, 0.1,
                                             OP.min, OP.mult))
                bld.wait(rec, "scalar", "V", qv)
                ga = bld.tok(rec, "scalar", "A",
                             lambda E, fc=fc, tj=tj:
                             E.activation(a[:, fc * NOWN + tj * TJ:
                                            fc * NOWN + (tj + 1) * TJ],
                                          zb[:, :], AF.Gelu_apprx_tanh))
                job += 1
            w1_tok[fc] = bld.c["T"]
        bld.wait(rec, "tensor", "A", bld.c["A"])
        ev2, w2_tok, st2 = {}, {}, {}
        for eo in range(EON):
            for fc in range(FCH):
                par = fc % 2
                if fc >= 2:
                    bld.wait(rec, "gpsimd", "T", w2_tok[(eo, fc - 2)])
                elif eo >= 1:
                    bld.wait(rec, "gpsimd", "T", w2_tok[(eo - 1, FCH - 2 + fc)])
                v = bld.dma(rec, "gpsimd", gW2[par], w2c[par][:, :],
                            env["w2T"][fc * 128:(fc + 1) * 128,
                                       eo * EOSZ:(eo + 1) * EOSZ])
                bld.wait(rec, "tensor", gW2[par], v)
                if fc == 0 and eo >= 1:
                    for tc in range(TCHO):
                        bld.wait(rec, "tensor", "V", ev2[(eo - 1, tc)])

                def w2mm(E, tc, par=par, fc=fc):
                    return E.matmul(bank(tc)[:, :EOSZ],
                                    a[:, fc * NOWN + tc * 128:
                                      fc * NOWN + tc * 128 + 128],
                                    w2c[par][:, :],
                                    start=(fc == 0), stop=(fc == FCH - 1))
                for tc in range(TCHO - 1):
                    bld.run(rec, "tensor", lambda E, tc=tc, f=w2mm: f(E, tc))
                w2_tok[(eo, fc)] = bld.tok(rec, "tensor", "T",
                                           lambda E, f=w2mm: f(E, TCHO - 1))
            mmt = bld.c["T"]
            for tc in range(TCHO):
                par = tc % 2
                r0 = g.R0 + tc * 128
                if tc >= 2:
                    bld.wait(rec, "gpsimd", "V", ev2[(eo, tc - 2)])
                elif eo >= 1:
                    bld.wait(rec, "gpsimd", "V", ev2[(eo - 1, TCHO - 2 + tc)])
                vx2 = bld.dma(rec, "gpsimd", gXR[par], xr[par][:, :],
                              env["x_slab"][r0:r0 + 128,
                                            eo * EOSZ:(eo + 1) * EOSZ])
                va2 = bld.dma(rec, "gpsimd", gXR[par], ar[par][:, :],
                              env["attn_sp"][tc * 128:(tc + 1) * 128,
                                             eo * EOSZ:(eo + 1) * EOSZ])
                bld.wait(rec, "vector", "T", mmt)
                bld.wait(rec, "vector", gXR[par], va2)
                if tc >= 2:
                    bld.wait(rec, "vector", xO[par], bld.c[xO[par]])
                elif eo >= 1:
                    bld.wait(rec, "vector", xO[par], bld.c[xO[par]])
                bld.run(rec, "vector", lambda E, tc=tc, par=par:
                        E.tensor_add(t3[:, :], bank(tc)[:, :EOSZ], xr[par][:, :]))
                bld.run(rec, "vector", lambda E, par=par:
                        E.tensor_add(t3[:, :], t3[:, :], ar[par][:, :]))
                ev2[(eo, tc)] = bld.tok(
                    rec, "vector", "V",
                    lambda E, par=par, eo=eo:
                    E.tensor_add(ob[par][:, :], t3[:, :],
                                 brp[:, eo * EOSZ:(eo + 1) * EOSZ]))
                bld.wait(rec, "sync", "V", ev2[(eo, tc)])
                bld.dma(rec, "sync", xO[par],
                        env["out"][tc * 128:(tc + 1) * 128,
                                   eo * EOSZ:(eo + 1) * EOSZ],
                        ob[par][:, :])
        bld.drain(rec)
        rec.replay(nc)



def kernel(**inputs):
    cfg = full_cfg()
    in_maps = prepare_in_maps(cfg, inputs)
    nc = build_nc(cfg)
    res = run_bass_kernel_spmd(nc, in_maps, core_ids=list(range(NCORES)))
    out = np.empty((cfg.B, cfg.L, cfg.D), dtype=np.float32)
    for c in range(NCORES):
        b = c // cfg.CPB
        t0 = cfg.NOWN * (c % cfg.CPB)
        out[b, t0:t0 + cfg.NOWN, :] = res.results[c]["out"]
    return out
